# version 02
# baseline (speedup 1.0000x reference)
import numpy as np
import ml_dtypes
from contextlib import ExitStack

import concourse.bass as bass
import concourse.mybir as mybir
from concourse.bass_utils import run_bass_kernel_spmd

F32 = mybir.dt.float32
BF16 = mybir.dt.bfloat16
U8 = mybir.dt.uint8
AF = mybir.ActivationFunctionType
OP = mybir.AluOpType
AX = mybir.AxisListType

D = 2048
NH = 16
HD = 128
DI = 8192
N_CORES = 8
TOK_PER_CORE = 2048
TILE = 128
G_TILES = 4
T_G = G_TILES * TILE
KS = D // 128
ISUBS = DI // 128
ICH = 8
N_ICH = ISUBS // ICH
QK_SCALE = 1.0 / np.sqrt(128.0)
EPS = 1e-5

ENGINES = ("tensor", "vector", "scalar", "gpsimd", "sync")


class Plan:

    def __init__(self):
        self.ops = {e: [] for e in ENGINES}
        self._last_wait = {}
        self.pads = {}

    def op(self, eng, fn):
        self.ops[eng].append(fn)

    def pad_op(self, eng):
        pad = self.pads.get(eng)
        if pad is None:
            return None
        if eng == "scalar":
            return lambda e, p=pad: e.copy(p, p)
        return lambda e, p=pad: e.memset(p, 0.0)

    def wait(self, eng, ev):
        if ev is None:
            return
        sem, val = ev
        if val <= 0:
            return
        key = (eng, sem.name)
        if self._last_wait.get(key, -1) >= val:
            return
        self._last_wait[key] = val
        self.ops[eng].append(lambda e, s=sem.h, v=val: e.wait_ge(s, v))
        if sem.name in ("s_pe", "s_sc", "s_ve", "s_gp") and eng in ("scalar", "vector"):
            p = self.pad_op(eng)
            if p is not None:
                self.ops[eng].append(p)

    def replay(self, eng, e):
        for fn in self.ops[eng]:
            fn(e)


class Sem:
    def __init__(self, nc, stack, name):
        self.h = stack.enter_context(nc.semaphore(name))
        self.name = name
        self.val = 0

    def inc(self, plan, eng, fn, n=1):
        self.val += n
        v = self.val
        pad = plan.pad_op(eng) if (n == 1 and eng in ("scalar", "vector")) else None
        if pad is None:
            plan.op(eng, lambda e, fn=fn, s=self.h, n=n: fn(e).then_inc(s, n))
        else:
            plan.op(eng, fn)
            plan.op(eng, lambda e, pad=pad, s=self.h, n=n: pad(e).then_inc(s, n))
        return (self, v)

    def ev(self):
        return (self, self.val)


class Bank:

    def __init__(self, t):
        self.t = t
        self.free_ev = None


DBG = set()
G_SKIP = set()


def build_program(n_groups=4, stop_after=None):
    nc = bass.Bass("TRN2", target_bir_lowering=False, detect_race_conditions=False)

    x_d = nc.dram_tensor("x", [TOK_PER_CORE, D], F32, kind="ExternalInput").ap()
    wqkv_d = nc.dram_tensor("wqkv", [D, 3 * D], F32, kind="ExternalInput").ap()
    wout_d = nc.dram_tensor("wout", [D, D], F32, kind="ExternalInput").ap()
    bout_d = nc.dram_tensor("bout", [128, D], F32, kind="ExternalInput").ap()
    gw_d = nc.dram_tensor("gw", [ISUBS, 128, D], U8, kind="ExternalInput").ap()
    uw_d = nc.dram_tensor("uw", [ISUBS, 128, D], U8, kind="ExternalInput").ap()
    dw_d = nc.dram_tensor("dw", [DI, D], U8, kind="ExternalInput").ap()
    gs_d = nc.dram_tensor("gs", [128, ISUBS], F32, kind="ExternalInput").ap()
    us_d = nc.dram_tensor("us", [128, ISUBS], F32, kind="ExternalInput").ap()
    ds_d = nc.dram_tensor("ds", [128, D], F32, kind="ExternalInput").ap()
    id_d = nc.dram_tensor("ident", [128, 128], BF16, kind="ExternalInput").ap()
    out_d = nc.dram_tensor("out", [TOK_PER_CORE, D], F32, kind="ExternalOutput").ap()
    r1_d = nc.dram_tensor("r1buf", [TOK_PER_CORE, D], F32, kind="Internal").ap()

    stack = ExitStack()
    sb = lambda name, shape, dtype: stack.enter_context(nc.sbuf_tensor(name, shape, dtype))

    def union(uname, nbytes, members):
        arena = stack.enter_context(nc.sbuf_tensor(uname, [128, nbytes], U8))
        base = nc.lookup_mloc(arena).addr
        return [nc.alloc_sbuf_tensor_at(name, list(shape), dtype, offset=base + off)
                for (name, shape, dtype, off) in members]

    ident = sb("identsb", [128, 128], BF16)
    gs = sb("gssb", [128, ISUBS], F32)
    us = sb("ussb", [128, ISUBS], F32)
    ones_c = sb("ones_c", [128, 1], BF16)

    spad = sb("spad", [128, 1], F32)
    vpad = sb("vpad", [128, 1], F32)
    P_pads = {"scalar": None, "vector": None}
    x_t = sb("x_t", [128, 2, D], F32)
    actT = sb("actT", [128, KS, T_G], BF16)
    Q = sb("Qb", [128, G_TILES, D], BF16)
    K = sb("Kb", [128, G_TILES, D], BF16)
    r1s = sb("r1s", [128, 2, D], F32)
    abf = sb("abf", [128, D], BF16)
    lnbf = sb("lnbf", [128, D], BF16)
    prodb = sb("prodb", [128, D], BF16)
    sqd = prodb
    scor = sb("scor", [128, NH * NH], F32)
    exps = sb("exps", [128, NH * NH], F32)
    Pm = sb("Pm", [128, NH * NH], F32)
    nmax = sb("nmax", [128, NH], F32)
    rden = sb("rden", [128, NH], F32)
    den = sb("den", [128, NH], F32)
    stats = sb("stats", [128, 2, 12], F32)
    csum = sb("csum", [128, G_TILES], F32)
    cneg = sb("cneg", [128, 1], F32)
    epsf = sb("epsf", [128, 1], F32)
    epsc = sb("epsc", [128, 1], F32)
    silb = sb("silb", [128, 2, T_G], BF16)
    ub = sb("ub", [128, 2, T_G], BF16)

    Vb, yb, dacc = union("u2", 32 * 1024, [
        ("Vb", [128, G_TILES, D], F32, 0),
        ("yb", [128, G_TILES, D], F32, 0),
        ("dacc", [128, G_TILES, D], F32, 0),
    ])
    h0T, gring, uring, acc, bout, ds, ytmp, otile = union("u1", 32 * 1024, [
        ("h0T", [128, 2, ICH, T_G], BF16, 0),
        ("gring", [128, 2, KS, 128], BF16, 16 * 1024),
        ("uring", [128, 2, KS, 128], BF16, 24 * 1024),
        ("accb", [128, 2, D], F32, 0),
        ("boutsb", [128, D], F32, 0),
        ("dssb", [128, D], F32, 8 * 1024),
        ("ytmp", [128, D], F32, 16 * 1024),
        ("otile", [128, D], F32, 24 * 1024),
    ])
    wqr, wor, dring = union("u3", 32 * 1024, [
        ("wqr", [128, 2, KS, 512], BF16, 0),
        ("wor", [128, 2, KS, 512], BF16, 0),
        ("dring", [128, 2, ICH // 2, D], BF16, 0),
    ])

    pb = [stack.enter_context(nc.psum_tensor(f"pb{i}", [128, 512], F32)) for i in range(8)]
    banks = [Bank(t) for t in pb[:7]]
    ptrb = Bank(pb[7])

    s_const = Sem(nc, stack, "s_const")
    s_bout = Sem(nc, stack, "s_bout")
    s_ds = Sem(nc, stack, "s_ds")
    s_x = [Sem(nc, stack, f"s_x{i}") for i in range(2)]
    s_u3 = [Sem(nc, stack, f"s_u3{i}") for i in range(2)]
    s_gr = [Sem(nc, stack, f"s_gr{i}") for i in range(2)]
    s_ur = [Sem(nc, stack, f"s_ur{i}") for i in range(2)]
    s_r1 = [Sem(nc, stack, f"s_r1{i}") for i in range(2)]
    s_pe = Sem(nc, stack, "s_pe")
    s_sc = Sem(nc, stack, "s_sc")
    s_ve = Sem(nc, stack, "s_ve")
    s_gp = Sem(nc, stack, "s_gp")
    s_st = Sem(nc, stack, "s_st")

    P = Plan()
    P.pads["scalar"] = spad[:, :]
    P.pads["vector"] = vpad[:, :]

    _init_bufs_v = [
        x_t[:, :, :], actT[:, :, :], Q[:, :, :], K[:, :, :], r1s[:, :, :],
        abf[:, :], lnbf[:, :], prodb[:, :], scor[:, :], exps[:, :], Pm[:, :],
        nmax[:, :], rden[:, :], den[:, :], stats[:, :, :], csum[:, :],
        cneg[:, :], silb[:, :, :], ub[:, :, :], Vb[:, :, :],
        ident[:, :], gs[:, :], us[:, :], ones_c[:, :],
    ]
    _init_bufs_s = [
        h0T[:, :, :, :], gring[:, :, :, :], uring[:, :, :, :],
        wqr[:, :, :, :], dring[:, :, :, :],
    ]
    for b in _init_bufs_v:
        P.op("vector", lambda e, b=b: e.memset(
            b.rearrange(" ".join(["p"] + [chr(97+i) for i in range(b.ndim-1)])
                        + " -> p (" + " ".join(chr(97+i) for i in range(b.ndim-1)) + ")")
            if b.ndim > 2 else b, 0.0))
    for b in _init_bufs_s:
        P.op("scalar", lambda e, b=b: e.memzero(
            b.rearrange(" ".join(["p"] + [chr(97+i) for i in range(b.ndim-1)])
                        + " -> p (" + " ".join(chr(97+i) for i in range(b.ndim-1)) + ")")
            if b.ndim > 2 else b))
    for bank in pb:
        P.op("vector", lambda e, b=bank: e.memset(b[:, :], 0.0))

    P.op("scalar", lambda e: e.memzero(spad[:, :]))
    iv = s_ve.inc(P, "vector", lambda e: e.memset(vpad[:, :], 0.0))
    P.wait("scalar", iv)
    iz = s_sc.inc(P, "scalar", lambda e: e.copy(spad[:, :], spad[:, :]))
    P.wait("vector", iz)
    for _ in range(80):
        P.op("scalar", P.pad_op("scalar"))
        P.op("vector", P.pad_op("vector"))
    iv2 = s_ve.inc(P, "vector", lambda e: e.tensor_copy(vpad[:, :], vpad[:, :]))
    i2 = s_sc.inc(P, "scalar", lambda e: e.copy(spad[:, :], spad[:, :]))
    P.wait("tensor", iv2)
    P.wait("sync", iv2)
    P.wait("gpsimd", iv2)
    P.wait("tensor", i2)
    P.wait("sync", i2)
    P.wait("gpsimd", i2)

    def ld(dst, src):
        return s_const.inc(P, "sync", lambda e, d=dst, s=src: e.dma_start(d, s), 16)

    ld(ident[:, :], id_d)
    ld(gs[:, :], gs_d)
    consts_ev = ld(us[:, :], us_d)
    ones_ev = s_ve.inc(P, "vector", lambda e: e.memset(ones_c[:, :], 1.0))
    P.op("vector", lambda e: e.memset(epsf[:, :], 1e-6))
    P.op("vector", lambda e: e.memset(epsc[:, :], EPS))
    P.wait("tensor", consts_ev)
    P.wait("tensor", ones_ev)
    P.wait("vector", consts_ev)
    P.wait("scalar", consts_ev)

    xslot_free = [None, None]
    accslot_free = [None, None]
    r1slot_free = [None, None]
    u3_free = [None, None]
    gring_free = [None, None]
    uring_free = [None, None]
    h0T_free = [None, None]
    silb_free = [None, None]
    lnbf_free = None
    abf_free = None
    ytmp_free = None
    otile_free = None

    def dump(name, ap, wait_ev):
        if name not in DBG:
            return None
        flat = ap if ap.ndim == 2 else ap.rearrange(
            " ".join(["p"] + [chr(97 + i) for i in range(ap.ndim - 1)])
            + " -> p (" + " ".join(chr(97 + i) for i in range(ap.ndim - 1)) + ")")
        d = nc.dram_tensor("dbg_" + name, [flat.shape[0], flat.shape[1]],
                           flat.dtype, kind="ExternalOutput").ap()
        P.wait("sync", wait_ev)
        return s_st.inc(P, "sync", lambda e, d=d, a=flat: e.dma_start(d, a), 16)

    def vreduce(fn):
        P.op("vector", P.pad_op("vector"))
        P.op("vector", fn)
        P.op("vector", P.pad_op("vector"))

    def layernorm(tl, buf_ap, src_ev_v, src_ev_s):
        sx = stats[:, tl, 0:1]
        ssq = stats[:, tl, 1:2]
        m = stats[:, tl, 2:3]
        mm = stats[:, tl, 3:4]
        negmm = stats[:, tl, 4:5]
        t0 = stats[:, tl, 5:6]
        lnv = stats[:, tl, 6:7]
        rstd = stats[:, tl, 7:8]
        bias = stats[:, tl, 8:9]
        vraw = stats[:, tl, 9:10]
        P.wait("vector", src_ev_v)
        vreduce(lambda e, o=sx, i=buf_ap: e.reduce_sum(o, i, AX.X))
        P.op("vector", lambda e, i=buf_ap: e.tensor_tensor(sqd[:, :], i, i, OP.mult))
        P.op("vector", P.pad_op("vector"))
        v_ev = s_ve.inc(P, "vector", lambda e, o=ssq: e.tensor_reduce(
            o, sqd[:, :], AX.X, OP.add))
        P.wait("scalar", v_ev)
        P.op("scalar", lambda e, o=m, i=sx: e.activation(o, i, AF.Identity, scale=1.0 / D))
        P.op("scalar", lambda e, o=mm, i=sx: e.activation(o, i, AF.Square, scale=1.0 / D))
        P.op("scalar", lambda e, o=negmm, i=mm: e.activation(o, i, AF.Identity, scale=-1.0))
        P.op("scalar", lambda e, o=t0, i=ssq: e.activation(o, i, AF.Identity,
                                                           scale=1.0 / D, bias=epsc[:, :]))
        P.op("scalar", lambda e, o=vraw, i=t0: e.activation(o, i, AF.Identity,
                                                            bias=negmm))
        P.op("scalar", lambda e, o=lnv, i=vraw: e.activation(o, i, AF.Ln, bias=epsf[:, :]))
        r_ev = s_sc.inc(P, "scalar", lambda e, o=rstd, i=lnv: e.activation(o, i, AF.Exp, scale=-0.5))
        P.wait("vector", r_ev)
        b_ev = s_ve.inc(P, "vector", lambda e, o=bias, i=m, r=rstd: e.tensor_scalar(
            o, i, -1.0, r, OP.mult, OP.mult))
        return rstd, bias, b_ev

    def transposes(src_bf_ap_fn, dst_chunk0, t_in_group, ready_ev_s):
        ev = None
        for b in range(4):
            P.wait("tensor", ready_ev_s)
            P.wait("tensor", ptrb.free_ev)
            t_ev = None
            for c in range(4):
                cc = b * 4 + c
                fn = (lambda e, o=ptrb.t[:, c * 128:(c + 1) * 128],
                      i=src_bf_ap_fn(cc), st=(c == 0), sp=(c == 3):
                      e.matmul(o, i, ident[:, :], start=st, stop=sp))
                if c == 3:
                    t_ev = s_pe.inc(P, "tensor", fn)
                else:
                    P.op("tensor", fn)
            P.wait("scalar", t_ev)
            out_ap = actT[:, b * 4:(b + 1) * 4, t_in_group * 128:(t_in_group + 1) * 128]
            ev = s_sc.inc(P, "scalar", lambda e, o=out_ap: e.copy(
                o, ptrb.t[:, :].rearrange("p (a b) -> p a b", b=128)))
            ptrb.free_ev = ev
        return ev

    ln2T_ev = None
    prev_g_ve = None
    prev_g_st = None
    for g in range(n_groups):
        base_tok = g * T_G
        if g > 0:
            for s in (s_pe, s_sc, s_ve, s_gp, s_st, s_bout, s_ds,
                      s_x[0], s_x[1], s_u3[0], s_u3[1], s_gr[0], s_gr[1],
                      s_ur[0], s_ur[1], s_r1[0], s_r1[1]):
                for eng in ENGINES:
                    P.wait(eng, s.ev())
            gb_v = s_ve.inc(P, "vector", lambda e: e.memset(vpad[:, :], 0.0))
            gb_s = s_sc.inc(P, "scalar", lambda e: e.copy(spad[:, :], spad[:, :]))
            for eng in ("scalar", "tensor", "sync", "gpsimd"):
                P.wait(eng, gb_v)
            for eng in ("vector", "tensor", "sync", "gpsimd"):
                P.wait(eng, gb_s)
            for _ in range(80):
                P.op("scalar", P.pad_op("scalar"))
                P.op("vector", P.pad_op("vector"))
            gb_v2 = s_ve.inc(P, "vector", lambda e: e.tensor_copy(vpad[:, :], vpad[:, :]))
            gb_s2 = s_sc.inc(P, "scalar", lambda e: e.copy(spad[:, :], spad[:, :]))
            for eng in ("scalar", "tensor", "sync", "gpsimd"):
                P.wait(eng, gb_v2)
            for eng in ("vector", "tensor", "sync", "gpsimd"):
                P.wait(eng, gb_s2)

        ln1_evs = []
        for t in range(G_TILES):
            sl = t % 2
            P.wait("sync", xslot_free[sl])
            xev_t = s_x[sl].inc(P, "sync", lambda e, sl=sl, r0=(base_tok + t * TILE):
                                e.dma_start(x_t[:, sl, :], x_d[r0:r0 + 128, :]), 16)
            ra_v = s_ve.inc(P, "vector", lambda e: e.memset(vpad[:, :], 0.0))
            ra_s = s_sc.inc(P, "scalar", lambda e: e.copy(spad[:, :], spad[:, :]))
            P.wait("vector", ra_s)
            P.wait("scalar", ra_v)
            P.wait("tensor", ra_v)
            P.wait("tensor", ra_s)
            if t == 0:
                dump(f"g{g}xA", x_t[:, 0, :], xev_t)
            rstd, bias, b_ev = layernorm(t % 2, x_t[:, sl, :], xev_t, xev_t)
            if t == 0:
                dump(f"g{g}stA", stats[:, :, :], b_ev)
                dump(f"g{g}sqdA", sqd[:, :], b_ev)
            P.wait("scalar", b_ev)
            P.wait("scalar", lnbf_free)
            n_ev = s_sc.inc(P, "scalar", lambda e, sl=sl, r=rstd, b=bias: e.activation(
                lnbf[:, :], x_t[:, sl, :], AF.Identity, scale=r, bias=b))
            xslot_free[sl] = n_ev
            if t == 0:
                d_ev = dump(f"g{g}ln1", lnbf[:, :], n_ev)
                if d_ev is not None:
                    P.wait("tensor", d_ev)
            tr_ev = transposes(lambda c: lnbf[:, c * 128:(c + 1) * 128], None, t, n_ev)
            lnbf_free = tr_ev
            ln1_evs.append(tr_ev)
        ln1T_ev = ln1_evs[-1]
        dump(f"g{g}ln1T", actT[:, :, :], ln1T_ev)
        if stop_after == "A":
            prev_g_ve = s_ve.ev()
            prev_g_st = s_st.ev()
            continue

        qk_done = None
        v_done = None
        for n in range(12):
            sl = n % 2
            P.wait("gpsimd", u3_free[sl])
            for hq in range(4):
                src = wqkv_d[hq * 512:(hq + 1) * 512, n * 512:(n + 1) * 512]
                src = src.rearrange("(hs p) c -> p hs c", p=128)
                dst = wqr[:, sl, hq * 4:(hq + 1) * 4, :]
                wev = s_u3[sl].inc(P, "gpsimd", lambda e, d=dst, s=src: e.dma_start(d, s), 16)
                P.wait("gpsimd", wev)
            bset = [banks[(n * 4 + t) % 7] for t in range(G_TILES)]
            P.wait("tensor", wev)
            P.wait("tensor", ln1T_ev)
            for t in range(G_TILES):
                P.wait("tensor", bset[t].free_ev)
            mm_ev = None
            for hs in range(KS):
                for t in range(G_TILES):
                    lhs = actT[:, hs, t * 128:(t + 1) * 128]
                    rhs = wqr[:, sl, hs, :]
                    fn = (lambda e, o=bset[t].t[:, :], l=lhs, r=rhs,
                          st=(hs == 0), sp=(hs == KS - 1):
                          e.matmul(o, l, r, start=st, stop=sp))
                    if hs == KS - 1 and t == G_TILES - 1:
                        mm_ev = s_pe.inc(P, "tensor", fn)
                    else:
                        P.op("tensor", fn)
            u3_free[sl] = mm_ev
            if n < 8:
                P.wait("scalar", mm_ev)
                for t in range(G_TILES):
                    if n < 4:
                        dst = Q[:, t, n * 512:(n + 1) * 512]
                        e_ev = s_sc.inc(P, "scalar", lambda e, o=dst, i=bset[t].t[:, :]:
                                        e.activation(o, i, AF.Copy, scale=QK_SCALE))
                    else:
                        dst = K[:, t, (n - 4) * 512:(n - 3) * 512]
                        e_ev = s_sc.inc(P, "scalar", lambda e, o=dst, i=bset[t].t[:, :]:
                                        e.copy(o, i))
                    bset[t].free_ev = e_ev
                if n == 7:
                    qk_done = e_ev
            else:
                P.wait("vector", mm_ev)
                for t in range(G_TILES):
                    dst = Vb[:, t, (n - 8) * 512:(n - 7) * 512]
                    e_ev = s_ve.inc(P, "vector", lambda e, o=dst, i=bset[t].t[:, :]:
                                    e.tensor_copy(o, i))
                    bset[t].free_ev = e_ev
                if n == 11:
                    v_done = e_ev
        dump(f"g{g}Q", Q[:, :, :], qk_done)
        dump(f"g{g}K", K[:, :, :], qk_done)
        dump(f"g{g}V", Vb[:, :, :], v_done)
        if stop_after == "B":
            prev_g_ve = s_ve.ev()
            prev_g_st = s_st.ev()
            continue

        attnT_evs = []
        for t in range(G_TILES):
            P.wait("vector", qk_done)
            sc_ev = None
            for h in range(NH):
                q_sl = Q[:, t, h * 128:(h + 1) * 128].unsqueeze(1).broadcast_to([128, NH, 128])
                k_sl = K[:, t, :].rearrange("p (H d) -> p H d", d=128)
                pr3 = prodb[:, :].rearrange("p (H d) -> p H d", d=128)
                P.op("vector", lambda e, o=pr3, a=q_sl, b=k_sl: e.tensor_tensor(o, a, b, OP.mult))
                if h == NH - 1:
                    P.op("vector", P.pad_op("vector"))
                    sc_ev = s_ve.inc(P, "vector", lambda e, o=scor[:, h * NH:(h + 1) * NH],
                                     i=pr3: e.tensor_reduce(o, i, AX.X, OP.add))
                else:
                    vreduce(lambda e, o=scor[:, h * NH:(h + 1) * NH], i=pr3:
                            e.tensor_reduce(o, i, AX.X, OP.add))
            sc3 = scor[:, :].rearrange("p (h H) -> p h H", H=NH)
            P.op("vector", lambda e: e.memset(prodb[:, :], 0.0))
            P.op("vector", P.pad_op("vector"))
            nm_ev = s_ve.inc(P, "vector", lambda e, o=nmax[:, :], i=sc3:
                             e.reduce_max(o, i, AX.X, negate=True))
            P.wait("scalar", nm_ev)
            ex_ev = None
            for h in range(NH):
                fn = (lambda e, o=exps[:, h * NH:(h + 1) * NH],
                      i=scor[:, h * NH:(h + 1) * NH], b=nmax[:, h:h + 1],
                      a=den[:, h:h + 1]:
                      e.activation(o, i, AF.Exp, bias=b, accum_out=a))
                if h == NH - 1:
                    ex_ev = s_sc.inc(P, "scalar", fn)
                else:
                    P.op("scalar", fn)
            P.wait("vector", ex_ev)
            P.op("vector", lambda e: e.reciprocal(rden[:, :], den[:, :]))
            P.op("vector", P.pad_op("vector"))
            asl = t % 2
            P.wait("vector", v_done)
            P.wait("vector", ex_ev)
            P.wait("vector", accslot_free[asl])
            for h in range(NH):
                adst = acc[:, asl, h * 128:(h + 1) * 128]
                for H in range(NH):
                    vsl = Vb[:, t, H * 128:(H + 1) * 128]
                    pcol = exps[:, h * NH + H:h * NH + H + 1]
                    if H == 0:
                        fn = (lambda e, o=adst, v=vsl, p=pcol:
                              e.tensor_scalar(o, v, p, None, OP.mult))
                    else:
                        fn = (lambda e, o=adst, v=vsl, p=pcol:
                              e.scalar_tensor_tensor(o, v, p, o, OP.mult, OP.add))
                    P.op("vector", fn)
            P.op("vector", P.pad_op("vector"))
            av_ev = None
            for h in range(NH):
                adst = acc[:, asl, h * 128:(h + 1) * 128]
                fn = (lambda e, o=adst, r=rden[:, h:h + 1]:
                      e.tensor_scalar(o, o, r, None, OP.mult))
                if h == NH - 1:
                    av_ev = s_ve.inc(P, "vector", fn)
                else:
                    P.op("vector", fn)
            if t == 0:
                d_ev = dump(f"g{g}acc", acc[:, asl, :], av_ev)
                if d_ev is not None:
                    P.wait("scalar", d_ev)
            P.wait("scalar", av_ev)
            P.wait("scalar", abf_free)
            cast_ev = s_sc.inc(P, "scalar", lambda e, i=acc[:, asl, :]:
                               e.copy(abf[:, :], i))
            accslot_free[asl] = cast_ev
            tr_ev = transposes(lambda c: abf[:, c * 128:(c + 1) * 128], None, t, cast_ev)
            abf_free = tr_ev
            attnT_evs.append(tr_ev)
        attnT_ev = attnT_evs[-1]
        acc_done_ev = cast_ev
        if stop_after == "C":
            prev_g_ve = s_ve.ev()
            prev_g_st = s_st.ev()
            continue

        P.wait("sync", acc_done_ev)
        bout_ev = s_bout.inc(P, "sync", lambda e: e.dma_start(bout[:, :], bout_d), 16)
        for n in range(4):
            sl = n % 2
            P.wait("gpsimd", u3_free[sl])
            for hq in range(4):
                src = wout_d[hq * 512:(hq + 1) * 512, n * 512:(n + 1) * 512]
                src = src.rearrange("(hs p) c -> p hs c", p=128)
                dst = wor[:, sl, hq * 4:(hq + 1) * 4, :]
                wev = s_u3[sl].inc(P, "gpsimd", lambda e, d=dst, s=src: e.dma_start(d, s), 16)
                P.wait("gpsimd", wev)
            bset = [banks[(n * 4 + t) % 7] for t in range(G_TILES)]
            P.wait("tensor", wev)
            P.wait("tensor", attnT_ev)
            for t in range(G_TILES):
                P.wait("tensor", bset[t].free_ev)
            mm_ev = None
            for hs in range(KS):
                for t in range(G_TILES):
                    lhs = actT[:, hs, t * 128:(t + 1) * 128]
                    rhs = wor[:, sl, hs, :]
                    fn = (lambda e, o=bset[t].t[:, :], l=lhs, r=rhs,
                          st=(hs == 0), sp=(hs == KS - 1):
                          e.matmul(o, l, r, start=st, stop=sp))
                    if hs == KS - 1 and t == G_TILES - 1:
                        mm_ev = s_pe.inc(P, "tensor", fn)
                    else:
                        P.op("tensor", fn)
            u3_free[sl] = mm_ev
            P.wait("scalar", mm_ev)
            for t in range(G_TILES):
                dst = yb[:, t, n * 512:(n + 1) * 512]
                e_ev = s_sc.inc(P, "scalar", lambda e, o=dst, i=bset[t].t[:, :]: e.copy(o, i))
                bset[t].free_ev = e_ev
            y_done = e_ev

        ln2_evs = []
        for t in range(G_TILES):
            P.wait("vector", y_done)
            P.wait("vector", bout_ev)
            ba_ev = s_ve.inc(P, "vector", lambda e, y=yb[:, t, :]:
                             e.tensor_tensor(y, y, bout[:, :], OP.add))
            rstd, bias, b_ev = layernorm(t % 2, yb[:, t, :], ba_ev, ba_ev)
            if t == 0:
                d_ev = dump(f"g{g}y0", yb[:, t, :], ba_ev)
                if d_ev is not None:
                    P.wait("scalar", d_ev)
                d_ev = dump(f"g{g}st0", stats[:, :, :], b_ev)
                if d_ev is not None:
                    P.wait("scalar", d_ev)
            sl = t % 2
            P.wait("sync", xslot_free[sl])
            x2ev = s_x[sl].inc(P, "sync", lambda e, sl=sl, r0=(base_tok + t * TILE):
                               e.dma_start(x_t[:, sl, :], x_d[r0:r0 + 128, :]), 16)
            P.wait("scalar", b_ev)
            nrm_ev = s_sc.inc(P, "scalar", lambda e, y=yb[:, t, :], r=rstd, b=bias:
                              e.activation(y, y, AF.Identity, scale=r, bias=b))
            rsl = t % 2
            P.wait("vector", nrm_ev)
            P.wait("vector", x2ev)
            P.wait("vector", r1slot_free[rsl])
            r1_ev = s_ve.inc(P, "vector", lambda e, o=r1s[:, rsl, :], a=yb[:, t, :],
                             b=x_t[:, sl, :]: e.tensor_tensor(o, a, b, OP.add))
            xslot_free[sl] = r1_ev
            rstd2, bias2, b2_ev = layernorm(t % 2, r1s[:, rsl, :], r1_ev, r1_ev)
            P.wait("scalar", b2_ev)
            P.wait("scalar", lnbf_free)
            n2_ev = s_sc.inc(P, "scalar", lambda e, i=r1s[:, rsl, :], r=rstd2, b=bias2:
                             e.activation(lnbf[:, :], i, AF.Identity, scale=r, bias=b))
            tr_ev = transposes(lambda c: lnbf[:, c * 128:(c + 1) * 128], None, t, n2_ev)
            lnbf_free = tr_ev
            ln2_evs.append(tr_ev)
            P.wait("sync", tr_ev)
            sp_ev = s_r1[rsl].inc(P, "sync", lambda e, i=r1s[:, rsl, :],
                                  r0=(base_tok + t * TILE): e.dma_start(r1_d[r0:r0 + 128, :], i), 16)
            r1slot_free[rsl] = sp_ev
        ln2T_ev = ln2_evs[-1]
        if stop_after == "D":
            prev_g_ve = s_ve.ev()
            prev_g_st = s_st.ev()
            continue

        dacc_ready = [None] * G_TILES
        c_ready = [None] * G_TILES
        first_ich = [True] * G_TILES
        P.wait("gpsimd", prev_g_ve)
        P.wait("gpsimd", prev_g_st)
        for ic in range(N_ICH):
            hsl = ic % 2
            for il in range(ICH):
                isub = ic * ICH + il
                gsl = isub % 2
                P.wait("gpsimd", gring_free[gsl])
                gev = s_gr[gsl].inc(P, "gpsimd", lambda e, d=gring[:, gsl, :, :], s=gw_d[isub]:
                                    e.dma_start(d.rearrange("p a b -> p (a b)"), s), 16)
                P.wait("gpsimd", gev)
                P.wait("gpsimd", uring_free[gsl])
                uev = s_ur[gsl].inc(P, "gpsimd", lambda e, d=uring[:, gsl, :, :], s=uw_d[isub]:
                                    e.dma_start(d.rearrange("p a b -> p (a b)"), s), 16)
                P.wait("gpsimd", uev)
                pg = banks[0 + 2 * (isub % 2)]
                pu = banks[1 + 2 * (isub % 2)]
                P.wait("tensor", gev)
                P.wait("tensor", uev)
                P.wait("tensor", ln2T_ev)
                P.wait("tensor", pg.free_ev)
                P.wait("tensor", pu.free_ev)
                if il == 0:
                    P.wait("tensor", h0T_free[hsl])
                for hs in range(KS):
                    P.op("tensor", lambda e, o=pg.t[:, :], l=gring[:, gsl, hs, :],
                         r=actT[:, hs, :], st=(hs == 0), sp=(hs == KS - 1):
                         e.matmul(o, l, r, start=st, stop=sp))
                gu_ev = None
                for hs in range(KS):
                    fn = (lambda e, o=pu.t[:, :], l=uring[:, gsl, hs, :],
                          r=actT[:, hs, :], st=(hs == 0), sp=(hs == KS - 1):
                          e.matmul(o, l, r, start=st, stop=sp))
                    if hs == KS - 1:
                        gu_ev = s_pe.inc(P, "tensor", fn)
                    else:
                        P.op("tensor", fn)
                gring_free[gsl] = gu_ev
                uring_free[gsl] = gu_ev
                ssl = isub % 2
                P.wait("scalar", gu_ev)
                P.wait("scalar", silb_free[ssl])
                P.op("scalar", lambda e, o=silb[:, ssl, :], i=pg.t[:, :],
                     sc=gs[:, isub:isub + 1]: e.activation(o, i, AF.Silu, scale=sc))
                su_ev = s_sc.inc(P, "scalar", lambda e, o=ub[:, ssl, :], i=pu.t[:, :],
                                 sc=us[:, isub:isub + 1]: e.activation(o, i, AF.Identity, scale=sc))
                pg.free_ev = su_ev
                pu.free_ev = su_ev
                P.wait("vector", su_ev)
                h_ev = s_ve.inc(P, "vector", lambda e, o=h0T[:, hsl, il, :],
                                a=silb[:, ssl, :], b=ub[:, ssl, :]:
                                e.tensor_tensor(o, a, b, OP.mult))
                silb_free[ssl] = h_ev
            h_done = h_ev

            dwev = [None, None]
            for half in range(2):
                dsl_r = (ic * 2 + half) % 2
                P.wait("gpsimd", u3_free[dsl_r])
                r0 = ic * ICH * 128 + half * 512
                src = dw_d[r0:r0 + 512, :].rearrange("(a p) c -> p a c", p=128)
                dwev[half] = s_u3[dsl_r].inc(P, "gpsimd", lambda e, d=dring[:, dsl_r, :, :], s=src:
                                             e.dma_start(d, s), 16)
                P.wait("gpsimd", dwev[half])
            P.wait("tensor", h_done)
            down_bank_rot = 0
            for t in range(G_TILES):
                pc = banks[6]
                P.wait("tensor", pc.free_ev)
                pc_ev = None
                for il in range(ICH):
                    fn = (lambda e, o=pc.t[:, 0:1], l=h0T[:, hsl, il, t * 128:(t + 1) * 128],
                          st=(il == 0), sp=(il == ICH - 1):
                          e.matmul(o, l, ones_c[:, :], start=st, stop=sp))
                    if il == ICH - 1:
                        pc_ev = s_pe.inc(P, "tensor", fn)
                    else:
                        P.op("tensor", fn)
                P.wait("vector", pc_ev)
                P.wait("vector", ones_ev)
                if first_ich[t]:
                    cv_ev = s_ve.inc(P, "vector", lambda e, o=csum[:, t:t + 1], i=pc.t[:, 0:1]:
                                     e.tensor_copy(o, i))
                else:
                    cv_ev = s_ve.inc(P, "vector", lambda e, o=csum[:, t:t + 1], i=pc.t[:, 0:1]:
                                     e.tensor_tensor(o, o, i, OP.add))
                pc.free_ev = cv_ev
                c_ready[t] = cv_ev
                for hf in range(4):
                    pd = banks[4 + down_bank_rot % 2]
                    down_bank_rot += 1
                    P.wait("tensor", pd.free_ev)
                    pd_ev = None
                    for il in range(ICH):
                        half = il // 4
                        dsl_r = (ic * 2 + half) % 2
                        P.wait("tensor", dwev[half])
                        fn = (lambda e, o=pd.t[:, :],
                              l=h0T[:, hsl, il, t * 128:(t + 1) * 128],
                              r=dring[:, dsl_r, il % 4, hf * 512:(hf + 1) * 512],
                              st=(il == 0), sp=(il == ICH - 1):
                              e.matmul(o, l, r, start=st, stop=sp))
                        if il == ICH - 1:
                            pd_ev = s_pe.inc(P, "tensor", fn)
                        else:
                            P.op("tensor", fn)
                    P.wait("vector", pd_ev)
                    ddst = dacc[:, t, hf * 512:(hf + 1) * 512]
                    if first_ich[t]:
                        a_ev = s_ve.inc(P, "vector", lambda e, o=ddst, i=pd.t[:, :]:
                                        e.tensor_copy(o, i))
                    else:
                        a_ev = s_ve.inc(P, "vector", lambda e, o=ddst, i=pd.t[:, :]:
                                        e.tensor_tensor(o, o, i, OP.add))
                    pd.free_ev = a_ev
                    dacc_ready[t] = a_ev
                first_ich[t] = False
            ich_ev = s_pe.ev()
            h0T_free[hsl] = ich_ev
            u3_free[0] = ich_ev
            u3_free[1] = ich_ev

        if stop_after == "F":
            prev_g_ve = s_ve.ev()
            prev_g_st = s_st.ev()
            continue

        P.wait("sync", s_pe.ev())
        ds_ev = None
        if "ds" not in G_SKIP:
            ds_ev = s_ds.inc(P, "sync", lambda e: e.dma_start(ds[:, :], ds_d), 16)
        for t in range(G_TILES):
            P.wait("scalar", c_ready[t])
            cn_ev = s_sc.inc(P, "scalar", lambda e, i=csum[:, t:t + 1]: e.activation(
                cneg[:, :], i, AF.Identity, scale=-128.0))
            P.wait("vector", dacc_ready[t])
            P.wait("vector", cn_ev)
            P.wait("vector", ds_ev)
            P.wait("vector", ytmp_free)
            y2_ev = s_ve.inc(P, "vector", lambda e, i=dacc[:, t, :]:
                             e.scalar_tensor_tensor(ytmp[:, :], i, cneg[:, :], ds[:, :],
                                                    OP.add, OP.mult))
            if "noln" not in G_SKIP:
                rstd3, bias3, b3_ev = layernorm(t % 2, ytmp[:, :], y2_ev, y2_ev)
            rsl = t % 2
            r1l_ev = None
            if "reload" not in G_SKIP:
                P.wait("sync", r1slot_free[rsl])
                r1l_ev = s_r1[rsl].inc(P, "sync", lambda e, sl=rsl, r0=(base_tok + t * TILE):
                                       e.dma_start(r1s[:, sl, :], r1_d[r0:r0 + 128, :]), 16)
            if "noln" not in G_SKIP:
                P.wait("scalar", b3_ev)
                n3_ev = s_sc.inc(P, "scalar", lambda e, r=rstd3, b=bias3: e.activation(
                    ytmp[:, :], ytmp[:, :], AF.Identity, scale=r, bias=b))
                P.wait("vector", n3_ev)
            else:
                P.wait("vector", y2_ev)
            P.wait("vector", r1l_ev)
            P.wait("vector", otile_free)
            o_ev = s_ve.inc(P, "vector", lambda e, a=r1s[:, rsl, :]:
                            e.tensor_tensor(otile[:, :], ytmp[:, :], a, OP.add))
            ytmp_free = o_ev
            r1slot_free[rsl] = o_ev
            if "store" not in G_SKIP:
                P.wait("sync", o_ev)
                st_ev = s_st.inc(P, "sync", lambda e, r0=(base_tok + t * TILE):
                                 e.dma_start(out_d[r0:r0 + 128, :], otile[:, :]), 16)
                otile_free = st_ev
            else:
                otile_free = o_ev
        prev_g_ve = s_ve.ev()
        prev_g_st = s_st.ev()

    P.wait("sync", s_st.ev())

    with nc.Block() as block:
        @block.tensor
        def _(e):
            P.replay("tensor", e)

        @block.vector
        def _(e):
            P.replay("vector", e)

        @block.scalar
        def _(e):
            P.replay("scalar", e)

        @block.gpsimd
        def _(e):
            P.replay("gpsimd", e)

        @block.sync
        def _(e):
            P.replay("sync", e)

    stack.close()
    return nc


def _host_layout(inputs):
    x = np.asarray(inputs["L0_attn_x"], dtype=np.float32)
    wqkv = np.ascontiguousarray(np.asarray(inputs["L0_attn_W_qkv"], dtype=np.float32))
    wout = np.ascontiguousarray(np.asarray(inputs["L0_attn_W_out"], dtype=np.float32))
    bout = np.asarray(inputs["L0_attn_b_out"], dtype=np.float32)
    gq = np.asarray(inputs["L0_ffn_gate_w"])
    uq = np.asarray(inputs["L0_ffn_up_w"])
    dq = np.asarray(inputs["L0_ffn_down_w"])
    gsc = np.asarray(inputs["L0_ffn_gate_scale"], dtype=np.float32)
    usc = np.asarray(inputs["L0_ffn_up_scale"], dtype=np.float32)
    dsc = np.asarray(inputs["L0_ffn_down_scale"], dtype=np.float32)

    def lowbyte(a):
        a = np.ascontiguousarray(a.astype("<i4", copy=False))
        return a.view(np.uint8).reshape(a.shape[0], a.shape[1], 4)[:, :, 0]

    gl = lowbyte(gq)
    ul = lowbyte(uq)
    dl = lowbyte(dq)

    g4 = gl.reshape(ISUBS, 128, KS, 128)
    gw2 = np.ascontiguousarray(g4.transpose(0, 3, 2, 1)).reshape(ISUBS, 128, D)
    u4 = ul.reshape(ISUBS, 128, KS, 128)
    uw2 = np.ascontiguousarray(u4.transpose(0, 3, 2, 1)).reshape(ISUBS, 128, D)
    dw2 = np.ascontiguousarray(dl.T)

    gs_t = np.ascontiguousarray(gsc.reshape(ISUBS, 128).T)
    us_t = np.ascontiguousarray(usc.reshape(ISUBS, 128).T)
    ds_b = np.ascontiguousarray(np.broadcast_to(dsc[None, :], (128, D)))
    bout_b = np.ascontiguousarray(np.broadcast_to(bout[None, :], (128, D)))
    ident = np.eye(128, dtype=ml_dtypes.bfloat16)

    xr = x.reshape(N_CORES * TOK_PER_CORE, D)
    shards = [np.ascontiguousarray(xr[c * TOK_PER_CORE:(c + 1) * TOK_PER_CORE])
              for c in range(N_CORES)]
    shared = dict(wqkv=wqkv, wout=wout, bout=bout_b, gw=gw2, uw=uw2, dw=dw2,
                  gs=gs_t, us=us_t, ds=ds_b, ident=ident)
    return shards, shared


def kernel(**inputs):
    shards, shared = _host_layout(inputs)
    nc = build_program()
    in_maps = [dict(x=shards[c], **shared) for c in range(N_CORES)]
    res = run_bass_kernel_spmd(nc, in_maps, core_ids=list(range(N_CORES)))
    out = np.concatenate([np.asarray(res.results[c]["out"]) for c in range(N_CORES)],
                         axis=0)
    return out.reshape(2, 8192, D).astype(np.float32)


# version 03
# speedup vs baseline: 1.1561x; 1.1561x over previous
import numpy as np
import ml_dtypes
from contextlib import ExitStack

import concourse.bass as bass
import concourse.mybir as mybir
from concourse.bass_utils import run_bass_kernel_spmd

F32 = mybir.dt.float32
BF16 = mybir.dt.bfloat16
U8 = mybir.dt.uint8
AF = mybir.ActivationFunctionType
OP = mybir.AluOpType
AX = mybir.AxisListType

D = 2048
NH = 16
HD = 128
DI = 8192
N_CORES = 8
TOK_PER_CORE = 2048
TILE = 128
G_TILES = 4
T_G = G_TILES * TILE
KS = D // 128
ISUBS = DI // 128
ICH = 8
N_ICH = ISUBS // ICH
QK_SCALE = 1.0 / np.sqrt(128.0)
EPS = 1e-5

ENGINES = ("tensor", "vector", "scalar", "gpsimd", "sync")


class Plan:

    def __init__(self):
        self.ops = {e: [] for e in ENGINES}
        self._last_wait = {}
        self.pads = {}

    def op(self, eng, fn):
        self.ops[eng].append(fn)

    def pad_op(self, eng):
        pad = self.pads.get(eng)
        if pad is None:
            return None
        if eng == "scalar":
            return lambda e, p=pad: e.copy(p, p)
        return lambda e, p=pad: e.memset(p, 0.0)

    def wait(self, eng, ev):
        if ev is None:
            return
        sem, val = ev
        if val <= 0:
            return
        key = (eng, sem.name)
        if self._last_wait.get(key, -1) >= val:
            return
        self._last_wait[key] = val
        self.ops[eng].append(lambda e, s=sem.h, v=val: e.wait_ge(s, v))
        if sem.name in ("s_pe", "s_sc", "s_ve", "s_gp") and eng in ("scalar", "vector"):
            p = self.pad_op(eng)
            if p is not None:
                self.ops[eng].append(p)

    def replay(self, eng, e):
        for fn in self.ops[eng]:
            fn(e)


class Sem:
    def __init__(self, nc, stack, name):
        self.h = stack.enter_context(nc.semaphore(name))
        self.name = name
        self.val = 0

    def inc(self, plan, eng, fn, n=1):
        self.val += n
        v = self.val
        pad = plan.pad_op(eng) if (n == 1 and eng in ("scalar", "vector")) else None
        if pad is None:
            plan.op(eng, lambda e, fn=fn, s=self.h, n=n: fn(e).then_inc(s, n))
        else:
            plan.op(eng, fn)
            plan.op(eng, lambda e, pad=pad, s=self.h, n=n: pad(e).then_inc(s, n))
        return (self, v)

    def ev(self):
        return (self, self.val)


class Bank:

    def __init__(self, t):
        self.t = t
        self.free_ev = None


DBG = set()
G_SKIP = set()


def build_program(n_groups=4, stop_after=None):
    nc = bass.Bass("TRN2", target_bir_lowering=False, detect_race_conditions=False)

    x_d = nc.dram_tensor("x", [TOK_PER_CORE, D], F32, kind="ExternalInput").ap()
    wqkv_d = nc.dram_tensor("wqkv", [D, 3 * D], F32, kind="ExternalInput").ap()
    wout_d = nc.dram_tensor("wout", [D, D], F32, kind="ExternalInput").ap()
    bout_d = nc.dram_tensor("bout", [128, D], F32, kind="ExternalInput").ap()
    gw_d = nc.dram_tensor("gw", [ISUBS, 128, D], U8, kind="ExternalInput").ap()
    uw_d = nc.dram_tensor("uw", [ISUBS, 128, D], U8, kind="ExternalInput").ap()
    dw_d = nc.dram_tensor("dw", [DI, D], U8, kind="ExternalInput").ap()
    gs_d = nc.dram_tensor("gs", [128, ISUBS], F32, kind="ExternalInput").ap()
    us_d = nc.dram_tensor("us", [128, ISUBS], F32, kind="ExternalInput").ap()
    ds_d = nc.dram_tensor("ds", [128, D], F32, kind="ExternalInput").ap()
    id_d = nc.dram_tensor("ident", [128, 128], BF16, kind="ExternalInput").ap()
    out_d = nc.dram_tensor("out", [TOK_PER_CORE, D], F32, kind="ExternalOutput").ap()
    r1_d = nc.dram_tensor("r1buf", [TOK_PER_CORE, D], F32, kind="Internal").ap()

    stack = ExitStack()
    sb = lambda name, shape, dtype: stack.enter_context(nc.sbuf_tensor(name, shape, dtype))

    def union(uname, nbytes, members):
        arena = stack.enter_context(nc.sbuf_tensor(uname, [128, nbytes], U8))
        base = nc.lookup_mloc(arena).addr
        return [nc.alloc_sbuf_tensor_at(name, list(shape), dtype, offset=base + off)
                for (name, shape, dtype, off) in members]

    ident = sb("identsb", [128, 128], BF16)
    gs = sb("gssb", [128, ISUBS], F32)
    us = sb("ussb", [128, ISUBS], F32)
    ones_c = sb("ones_c", [128, 1], BF16)

    spad = sb("spad", [128, 1], F32)
    vpad = sb("vpad", [128, 1], F32)
    P_pads = {"scalar": None, "vector": None}
    x_t = sb("x_t", [128, 2, D], F32)
    actT = sb("actT", [128, KS, T_G], BF16)
    Q = sb("Qb", [128, G_TILES, D], BF16)
    K = sb("Kb", [128, G_TILES, D], BF16)
    r1s = sb("r1s", [128, 2, D], F32)
    abf = sb("abf", [128, D], BF16)
    lnbf = sb("lnbf", [128, D], BF16)
    prodb = sb("prodb", [128, D], BF16)
    sqd = prodb
    scor = sb("scor", [128, NH * NH], F32)
    exps = sb("exps", [128, NH * NH], F32)
    Pm = sb("Pm", [128, NH * NH], F32)
    nmax = sb("nmax", [128, NH], F32)
    rden = sb("rden", [128, NH], F32)
    den = sb("den", [128, NH], F32)
    stats = sb("stats", [128, 2, 12], F32)
    csum = sb("csum", [128, G_TILES], F32)
    cneg = sb("cneg", [128, 1], F32)
    epsf = sb("epsf", [128, 1], F32)
    epsc = sb("epsc", [128, 1], F32)
    silb = sb("silb", [128, 2, T_G], BF16)
    ub = sb("ub", [128, 2, T_G], BF16)

    Vb, yb, dacc = union("u2", 32 * 1024, [
        ("Vb", [128, G_TILES, D], F32, 0),
        ("yb", [128, G_TILES, D], F32, 0),
        ("dacc", [128, G_TILES, D], F32, 0),
    ])
    h0T, gring, uring, acc, bout, ds, ytmp, otile = union("u1", 32 * 1024, [
        ("h0T", [128, 2, ICH, T_G], BF16, 0),
        ("gring", [128, 2, KS, 128], BF16, 16 * 1024),
        ("uring", [128, 2, KS, 128], BF16, 24 * 1024),
        ("accb", [128, 2, D], F32, 0),
        ("boutsb", [128, D], F32, 0),
        ("dssb", [128, D], F32, 8 * 1024),
        ("ytmp", [128, D], F32, 16 * 1024),
        ("otile", [128, D], F32, 24 * 1024),
    ])
    wqr, wor, dring = union("u3", 32 * 1024, [
        ("wqr", [128, 2, KS, 512], BF16, 0),
        ("wor", [128, 2, KS, 512], BF16, 0),
        ("dring", [128, 2, ICH // 2, D], BF16, 0),
    ])

    pb = [stack.enter_context(nc.psum_tensor(f"pb{i}", [128, 512], F32)) for i in range(8)]
    banks = [Bank(t) for t in pb[:7]]
    ptrb = Bank(pb[7])

    s_const = Sem(nc, stack, "s_const")
    s_bout = Sem(nc, stack, "s_bout")
    s_ds = Sem(nc, stack, "s_ds")
    s_x = [Sem(nc, stack, f"s_x{i}") for i in range(2)]
    s_u3 = [Sem(nc, stack, f"s_u3{i}") for i in range(2)]
    s_gr = [Sem(nc, stack, f"s_gr{i}") for i in range(2)]
    s_ur = [Sem(nc, stack, f"s_ur{i}") for i in range(2)]
    s_r1 = [Sem(nc, stack, f"s_r1{i}") for i in range(2)]
    s_pe = Sem(nc, stack, "s_pe")
    s_sc = Sem(nc, stack, "s_sc")
    s_ve = Sem(nc, stack, "s_ve")
    s_gp = Sem(nc, stack, "s_gp")
    s_st = Sem(nc, stack, "s_st")

    P = Plan()
    P.pads["scalar"] = spad[:, :]
    P.pads["vector"] = vpad[:, :]

    _init_bufs_v = [
        x_t[:, :, :], actT[:, :, :], Q[:, :, :], K[:, :, :], r1s[:, :, :],
        abf[:, :], lnbf[:, :], prodb[:, :], scor[:, :], exps[:, :], Pm[:, :],
        nmax[:, :], rden[:, :], den[:, :], stats[:, :, :], csum[:, :],
        cneg[:, :], silb[:, :, :], ub[:, :, :], Vb[:, :, :],
        ident[:, :], gs[:, :], us[:, :], ones_c[:, :],
    ]
    _init_bufs_s = [
        h0T[:, :, :, :], gring[:, :, :, :], uring[:, :, :, :],
        wqr[:, :, :, :], dring[:, :, :, :],
    ]
    for b in _init_bufs_v:
        P.op("vector", lambda e, b=b: e.memset(
            b.rearrange(" ".join(["p"] + [chr(97+i) for i in range(b.ndim-1)])
                        + " -> p (" + " ".join(chr(97+i) for i in range(b.ndim-1)) + ")")
            if b.ndim > 2 else b, 0.0))
    for b in _init_bufs_s:
        P.op("scalar", lambda e, b=b: e.memzero(
            b.rearrange(" ".join(["p"] + [chr(97+i) for i in range(b.ndim-1)])
                        + " -> p (" + " ".join(chr(97+i) for i in range(b.ndim-1)) + ")")
            if b.ndim > 2 else b))
    for bank in pb:
        P.op("vector", lambda e, b=bank: e.memset(b[:, :], 0.0))

    P.op("scalar", lambda e: e.memzero(spad[:, :]))
    iv = s_ve.inc(P, "vector", lambda e: e.memset(vpad[:, :], 0.0))
    P.wait("scalar", iv)
    iz = s_sc.inc(P, "scalar", lambda e: e.copy(spad[:, :], spad[:, :]))
    P.wait("vector", iz)
    for _ in range(80):
        P.op("scalar", P.pad_op("scalar"))
        P.op("vector", P.pad_op("vector"))
    iv2 = s_ve.inc(P, "vector", lambda e: e.tensor_copy(vpad[:, :], vpad[:, :]))
    i2 = s_sc.inc(P, "scalar", lambda e: e.copy(spad[:, :], spad[:, :]))
    P.wait("tensor", iv2)
    P.wait("sync", iv2)
    P.wait("gpsimd", iv2)
    P.wait("tensor", i2)
    P.wait("sync", i2)
    P.wait("gpsimd", i2)

    def ld(dst, src):
        return s_const.inc(P, "sync", lambda e, d=dst, s=src: e.dma_start(d, s), 16)

    ld(ident[:, :], id_d)
    ld(gs[:, :], gs_d)
    consts_ev = ld(us[:, :], us_d)
    ones_ev = s_ve.inc(P, "vector", lambda e: e.memset(ones_c[:, :], 1.0))
    P.op("vector", lambda e: e.memset(epsf[:, :], 1e-6))
    P.op("vector", lambda e: e.memset(epsc[:, :], EPS))
    P.wait("tensor", consts_ev)
    P.wait("tensor", ones_ev)
    P.wait("vector", consts_ev)
    P.wait("scalar", consts_ev)

    xslot_free = [None, None]
    accslot_free = [None, None]
    r1slot_free = [None, None]
    u3_free = [None, None]
    gring_free = [None, None]
    uring_free = [None, None]
    h0T_free = [None, None]
    silb_free = [None, None]
    lnbf_free = None
    abf_free = None
    ytmp_free = None
    otile_free = None

    def dump(name, ap, wait_ev):
        if name not in DBG:
            return None
        flat = ap if ap.ndim == 2 else ap.rearrange(
            " ".join(["p"] + [chr(97 + i) for i in range(ap.ndim - 1)])
            + " -> p (" + " ".join(chr(97 + i) for i in range(ap.ndim - 1)) + ")")
        d = nc.dram_tensor("dbg_" + name, [flat.shape[0], flat.shape[1]],
                           flat.dtype, kind="ExternalOutput").ap()
        P.wait("sync", wait_ev)
        return s_st.inc(P, "sync", lambda e, d=d, a=flat: e.dma_start(d, a), 16)

    def vreduce(fn):
        P.op("vector", P.pad_op("vector"))
        P.op("vector", fn)
        P.op("vector", P.pad_op("vector"))

    def layernorm(tl, buf_ap, src_ev_v, src_ev_s):
        sx = stats[:, tl, 0:1]
        ssq = stats[:, tl, 1:2]
        m = stats[:, tl, 2:3]
        mm = stats[:, tl, 3:4]
        negmm = stats[:, tl, 4:5]
        t0 = stats[:, tl, 5:6]
        lnv = stats[:, tl, 6:7]
        rstd = stats[:, tl, 7:8]
        bias = stats[:, tl, 8:9]
        vraw = stats[:, tl, 9:10]
        P.wait("vector", src_ev_v)
        vreduce(lambda e, o=sx, i=buf_ap: e.reduce_sum(o, i, AX.X))
        P.op("vector", lambda e, i=buf_ap: e.tensor_tensor(sqd[:, :], i, i, OP.mult))
        P.op("vector", P.pad_op("vector"))
        v_ev = s_ve.inc(P, "vector", lambda e, o=ssq: e.tensor_reduce(
            o, sqd[:, :], AX.X, OP.add))
        P.wait("scalar", v_ev)
        P.op("scalar", lambda e, o=m, i=sx: e.activation(o, i, AF.Identity, scale=1.0 / D))
        P.op("scalar", lambda e, o=mm, i=sx: e.activation(o, i, AF.Square, scale=1.0 / D))
        P.op("scalar", lambda e, o=negmm, i=mm: e.activation(o, i, AF.Identity, scale=-1.0))
        P.op("scalar", lambda e, o=t0, i=ssq: e.activation(o, i, AF.Identity,
                                                           scale=1.0 / D, bias=epsc[:, :]))
        P.op("scalar", lambda e, o=vraw, i=t0: e.activation(o, i, AF.Identity,
                                                            bias=negmm))
        P.op("scalar", lambda e, o=lnv, i=vraw: e.activation(o, i, AF.Ln, bias=epsf[:, :]))
        r_ev = s_sc.inc(P, "scalar", lambda e, o=rstd, i=lnv: e.activation(o, i, AF.Exp, scale=-0.5))
        P.wait("vector", r_ev)
        b_ev = s_ve.inc(P, "vector", lambda e, o=bias, i=m, r=rstd: e.tensor_scalar(
            o, i, -1.0, r, OP.mult, OP.mult))
        return rstd, bias, b_ev

    def transposes(src_bf_ap_fn, dst_chunk0, t_in_group, ready_ev_s):
        ev = None
        for b in range(4):
            P.wait("tensor", ready_ev_s)
            P.wait("tensor", ptrb.free_ev)
            t_ev = None
            for c in range(4):
                cc = b * 4 + c
                fn = (lambda e, o=ptrb.t[:, c * 128:(c + 1) * 128],
                      i=src_bf_ap_fn(cc), st=(c == 0), sp=(c == 3):
                      e.matmul(o, i, ident[:, :], start=st, stop=sp))
                if c == 3:
                    t_ev = s_pe.inc(P, "tensor", fn)
                else:
                    P.op("tensor", fn)
            P.wait("scalar", t_ev)
            out_ap = actT[:, b * 4:(b + 1) * 4, t_in_group * 128:(t_in_group + 1) * 128]
            ev = s_sc.inc(P, "scalar", lambda e, o=out_ap: e.copy(
                o, ptrb.t[:, :].rearrange("p (a b) -> p a b", b=128)))
            ptrb.free_ev = ev
        return ev

    ln2T_ev = None
    prev_g_ve = None
    prev_g_st = None
    for g in range(n_groups):
        base_tok = g * T_G
        if g > 0:
            for s in (s_pe, s_sc, s_ve, s_gp, s_st, s_bout, s_ds,
                      s_x[0], s_x[1], s_u3[0], s_u3[1], s_gr[0], s_gr[1],
                      s_ur[0], s_ur[1], s_r1[0], s_r1[1]):
                for eng in ENGINES:
                    P.wait(eng, s.ev())
            gb_v = s_ve.inc(P, "vector", lambda e: e.memset(vpad[:, :], 0.0))
            gb_s = s_sc.inc(P, "scalar", lambda e: e.copy(spad[:, :], spad[:, :]))
            for eng in ("scalar", "tensor", "sync", "gpsimd"):
                P.wait(eng, gb_v)
            for eng in ("vector", "tensor", "sync", "gpsimd"):
                P.wait(eng, gb_s)

        ln1_evs = []
        for t in range(G_TILES):
            sl = t % 2
            P.wait("sync", xslot_free[sl])
            xev_t = s_x[sl].inc(P, "sync", lambda e, sl=sl, r0=(base_tok + t * TILE):
                                e.dma_start(x_t[:, sl, :], x_d[r0:r0 + 128, :]), 16)
            ra_v = s_ve.inc(P, "vector", lambda e: e.memset(vpad[:, :], 0.0))
            ra_s = s_sc.inc(P, "scalar", lambda e: e.copy(spad[:, :], spad[:, :]))
            P.wait("vector", ra_s)
            P.wait("scalar", ra_v)
            P.wait("tensor", ra_v)
            P.wait("tensor", ra_s)
            if t == 0:
                dump(f"g{g}xA", x_t[:, 0, :], xev_t)
            rstd, bias, b_ev = layernorm(t % 2, x_t[:, sl, :], xev_t, xev_t)
            if t == 0:
                dump(f"g{g}stA", stats[:, :, :], b_ev)
                dump(f"g{g}sqdA", sqd[:, :], b_ev)
            P.wait("scalar", b_ev)
            P.wait("scalar", lnbf_free)
            n_ev = s_sc.inc(P, "scalar", lambda e, sl=sl, r=rstd, b=bias: e.activation(
                lnbf[:, :], x_t[:, sl, :], AF.Identity, scale=r, bias=b))
            xslot_free[sl] = n_ev
            if t == 0:
                d_ev = dump(f"g{g}ln1", lnbf[:, :], n_ev)
                if d_ev is not None:
                    P.wait("tensor", d_ev)
            tr_ev = transposes(lambda c: lnbf[:, c * 128:(c + 1) * 128], None, t, n_ev)
            lnbf_free = tr_ev
            ln1_evs.append(tr_ev)
        ln1T_ev = ln1_evs[-1]
        dump(f"g{g}ln1T", actT[:, :, :], ln1T_ev)
        if stop_after == "A":
            prev_g_ve = s_ve.ev()
            prev_g_st = s_st.ev()
            continue

        qk_done = None
        v_done = None
        for n in range(12):
            sl = n % 2
            P.wait("gpsimd", u3_free[sl])
            for hq in range(4):
                src = wqkv_d[hq * 512:(hq + 1) * 512, n * 512:(n + 1) * 512]
                src = src.rearrange("(hs p) c -> p hs c", p=128)
                dst = wqr[:, sl, hq * 4:(hq + 1) * 4, :]
                wev = s_u3[sl].inc(P, "gpsimd", lambda e, d=dst, s=src: e.dma_start(d, s), 16)
            bset = [banks[(n * 4 + t) % 7] for t in range(G_TILES)]
            P.wait("tensor", wev)
            P.wait("tensor", ln1T_ev)
            for t in range(G_TILES):
                P.wait("tensor", bset[t].free_ev)
            mm_ev = None
            for hs in range(KS):
                for t in range(G_TILES):
                    lhs = actT[:, hs, t * 128:(t + 1) * 128]
                    rhs = wqr[:, sl, hs, :]
                    fn = (lambda e, o=bset[t].t[:, :], l=lhs, r=rhs,
                          st=(hs == 0), sp=(hs == KS - 1):
                          e.matmul(o, l, r, start=st, stop=sp))
                    if hs == KS - 1 and t == G_TILES - 1:
                        mm_ev = s_pe.inc(P, "tensor", fn)
                    else:
                        P.op("tensor", fn)
            u3_free[sl] = mm_ev
            if n < 8:
                P.wait("scalar", mm_ev)
                for t in range(G_TILES):
                    if n < 4:
                        dst = Q[:, t, n * 512:(n + 1) * 512]
                        e_ev = s_sc.inc(P, "scalar", lambda e, o=dst, i=bset[t].t[:, :]:
                                        e.activation(o, i, AF.Copy, scale=QK_SCALE))
                    else:
                        dst = K[:, t, (n - 4) * 512:(n - 3) * 512]
                        e_ev = s_sc.inc(P, "scalar", lambda e, o=dst, i=bset[t].t[:, :]:
                                        e.copy(o, i))
                    bset[t].free_ev = e_ev
                if n == 7:
                    qk_done = e_ev
            else:
                P.wait("vector", mm_ev)
                for t in range(G_TILES):
                    dst = Vb[:, t, (n - 8) * 512:(n - 7) * 512]
                    e_ev = s_ve.inc(P, "vector", lambda e, o=dst, i=bset[t].t[:, :]:
                                    e.tensor_copy(o, i))
                    bset[t].free_ev = e_ev
                if n == 11:
                    v_done = e_ev
        dump(f"g{g}Q", Q[:, :, :], qk_done)
        dump(f"g{g}K", K[:, :, :], qk_done)
        dump(f"g{g}V", Vb[:, :, :], v_done)
        if stop_after == "B":
            prev_g_ve = s_ve.ev()
            prev_g_st = s_st.ev()
            continue

        attnT_evs = []
        for t in range(G_TILES):
            P.wait("vector", qk_done)
            sc_ev = None
            for h in range(NH):
                q_sl = Q[:, t, h * 128:(h + 1) * 128].unsqueeze(1).broadcast_to([128, NH, 128])
                k_sl = K[:, t, :].rearrange("p (H d) -> p H d", d=128)
                pr3 = prodb[:, :].rearrange("p (H d) -> p H d", d=128)
                P.op("vector", lambda e, o=pr3, a=q_sl, b=k_sl: e.tensor_tensor(o, a, b, OP.mult))
                if h == NH - 1:
                    P.op("vector", P.pad_op("vector"))
                    sc_ev = s_ve.inc(P, "vector", lambda e, o=scor[:, h * NH:(h + 1) * NH],
                                     i=pr3: e.tensor_reduce(o, i, AX.X, OP.add))
                else:
                    vreduce(lambda e, o=scor[:, h * NH:(h + 1) * NH], i=pr3:
                            e.tensor_reduce(o, i, AX.X, OP.add))
            sc3 = scor[:, :].rearrange("p (h H) -> p h H", H=NH)
            P.op("vector", lambda e: e.memset(prodb[:, :], 0.0))
            P.op("vector", P.pad_op("vector"))
            nm_ev = s_ve.inc(P, "vector", lambda e, o=nmax[:, :], i=sc3:
                             e.reduce_max(o, i, AX.X, negate=True))
            P.wait("scalar", nm_ev)
            ex_ev = None
            for h in range(NH):
                fn = (lambda e, o=exps[:, h * NH:(h + 1) * NH],
                      i=scor[:, h * NH:(h + 1) * NH], b=nmax[:, h:h + 1],
                      a=den[:, h:h + 1]:
                      e.activation(o, i, AF.Exp, bias=b, accum_out=a))
                if h == NH - 1:
                    ex_ev = s_sc.inc(P, "scalar", fn)
                else:
                    P.op("scalar", fn)
            P.wait("vector", ex_ev)
            P.op("vector", lambda e: e.reciprocal(rden[:, :], den[:, :]))
            P.op("vector", P.pad_op("vector"))
            asl = t % 2
            P.wait("vector", v_done)
            P.wait("vector", ex_ev)
            P.wait("vector", accslot_free[asl])
            for h in range(NH):
                adst = acc[:, asl, h * 128:(h + 1) * 128]
                for H in range(NH):
                    vsl = Vb[:, t, H * 128:(H + 1) * 128]
                    pcol = exps[:, h * NH + H:h * NH + H + 1]
                    if H == 0:
                        fn = (lambda e, o=adst, v=vsl, p=pcol:
                              e.tensor_scalar(o, v, p, None, OP.mult))
                    else:
                        fn = (lambda e, o=adst, v=vsl, p=pcol:
                              e.scalar_tensor_tensor(o, v, p, o, OP.mult, OP.add))
                    P.op("vector", fn)
            P.op("vector", P.pad_op("vector"))
            av_ev = None
            for h in range(NH):
                adst = acc[:, asl, h * 128:(h + 1) * 128]
                fn = (lambda e, o=adst, r=rden[:, h:h + 1]:
                      e.tensor_scalar(o, o, r, None, OP.mult))
                if h == NH - 1:
                    av_ev = s_ve.inc(P, "vector", fn)
                else:
                    P.op("vector", fn)
            if t == 0:
                d_ev = dump(f"g{g}acc", acc[:, asl, :], av_ev)
                if d_ev is not None:
                    P.wait("scalar", d_ev)
            P.wait("scalar", av_ev)
            P.wait("scalar", abf_free)
            cast_ev = s_sc.inc(P, "scalar", lambda e, i=acc[:, asl, :]:
                               e.copy(abf[:, :], i))
            accslot_free[asl] = cast_ev
            tr_ev = transposes(lambda c: abf[:, c * 128:(c + 1) * 128], None, t, cast_ev)
            abf_free = tr_ev
            attnT_evs.append(tr_ev)
        attnT_ev = attnT_evs[-1]
        acc_done_ev = cast_ev
        if stop_after == "C":
            prev_g_ve = s_ve.ev()
            prev_g_st = s_st.ev()
            continue

        P.wait("sync", acc_done_ev)
        bout_ev = s_bout.inc(P, "sync", lambda e: e.dma_start(bout[:, :], bout_d), 16)
        for n in range(4):
            sl = n % 2
            P.wait("gpsimd", u3_free[sl])
            for hq in range(4):
                src = wout_d[hq * 512:(hq + 1) * 512, n * 512:(n + 1) * 512]
                src = src.rearrange("(hs p) c -> p hs c", p=128)
                dst = wor[:, sl, hq * 4:(hq + 1) * 4, :]
                wev = s_u3[sl].inc(P, "gpsimd", lambda e, d=dst, s=src: e.dma_start(d, s), 16)
            bset = [banks[(n * 4 + t) % 7] for t in range(G_TILES)]
            P.wait("tensor", wev)
            P.wait("tensor", attnT_ev)
            for t in range(G_TILES):
                P.wait("tensor", bset[t].free_ev)
            mm_ev = None
            for hs in range(KS):
                for t in range(G_TILES):
                    lhs = actT[:, hs, t * 128:(t + 1) * 128]
                    rhs = wor[:, sl, hs, :]
                    fn = (lambda e, o=bset[t].t[:, :], l=lhs, r=rhs,
                          st=(hs == 0), sp=(hs == KS - 1):
                          e.matmul(o, l, r, start=st, stop=sp))
                    if hs == KS - 1 and t == G_TILES - 1:
                        mm_ev = s_pe.inc(P, "tensor", fn)
                    else:
                        P.op("tensor", fn)
            u3_free[sl] = mm_ev
            P.wait("scalar", mm_ev)
            for t in range(G_TILES):
                dst = yb[:, t, n * 512:(n + 1) * 512]
                e_ev = s_sc.inc(P, "scalar", lambda e, o=dst, i=bset[t].t[:, :]: e.copy(o, i))
                bset[t].free_ev = e_ev
            y_done = e_ev

        ln2_evs = []
        for t in range(G_TILES):
            P.wait("vector", y_done)
            P.wait("vector", bout_ev)
            ba_ev = s_ve.inc(P, "vector", lambda e, y=yb[:, t, :]:
                             e.tensor_tensor(y, y, bout[:, :], OP.add))
            rstd, bias, b_ev = layernorm(t % 2, yb[:, t, :], ba_ev, ba_ev)
            if t == 0:
                d_ev = dump(f"g{g}y0", yb[:, t, :], ba_ev)
                if d_ev is not None:
                    P.wait("scalar", d_ev)
                d_ev = dump(f"g{g}st0", stats[:, :, :], b_ev)
                if d_ev is not None:
                    P.wait("scalar", d_ev)
            sl = t % 2
            P.wait("sync", xslot_free[sl])
            x2ev = s_x[sl].inc(P, "sync", lambda e, sl=sl, r0=(base_tok + t * TILE):
                               e.dma_start(x_t[:, sl, :], x_d[r0:r0 + 128, :]), 16)
            P.wait("scalar", b_ev)
            nrm_ev = s_sc.inc(P, "scalar", lambda e, y=yb[:, t, :], r=rstd, b=bias:
                              e.activation(y, y, AF.Identity, scale=r, bias=b))
            rsl = t % 2
            P.wait("vector", nrm_ev)
            P.wait("vector", x2ev)
            P.wait("vector", r1slot_free[rsl])
            r1_ev = s_ve.inc(P, "vector", lambda e, o=r1s[:, rsl, :], a=yb[:, t, :],
                             b=x_t[:, sl, :]: e.tensor_tensor(o, a, b, OP.add))
            xslot_free[sl] = r1_ev
            rstd2, bias2, b2_ev = layernorm(t % 2, r1s[:, rsl, :], r1_ev, r1_ev)
            P.wait("scalar", b2_ev)
            P.wait("scalar", lnbf_free)
            n2_ev = s_sc.inc(P, "scalar", lambda e, i=r1s[:, rsl, :], r=rstd2, b=bias2:
                             e.activation(lnbf[:, :], i, AF.Identity, scale=r, bias=b))
            tr_ev = transposes(lambda c: lnbf[:, c * 128:(c + 1) * 128], None, t, n2_ev)
            lnbf_free = tr_ev
            ln2_evs.append(tr_ev)
            P.wait("sync", tr_ev)
            sp_ev = s_r1[rsl].inc(P, "sync", lambda e, i=r1s[:, rsl, :],
                                  r0=(base_tok + t * TILE): e.dma_start(r1_d[r0:r0 + 128, :], i), 16)
            r1slot_free[rsl] = sp_ev
        ln2T_ev = ln2_evs[-1]
        if stop_after == "D":
            prev_g_ve = s_ve.ev()
            prev_g_st = s_st.ev()
            continue

        dacc_ready = [None] * G_TILES
        c_ready = [None] * G_TILES
        first_ich = [True] * G_TILES
        P.wait("gpsimd", prev_g_ve)
        P.wait("gpsimd", prev_g_st)
        for ic in range(N_ICH):
            hsl = ic % 2
            for il in range(ICH):
                isub = ic * ICH + il
                gsl = isub % 2
                P.wait("gpsimd", gring_free[gsl])
                gev = s_gr[gsl].inc(P, "gpsimd", lambda e, d=gring[:, gsl, :, :], s=gw_d[isub]:
                                    e.dma_start(d.rearrange("p a b -> p (a b)"), s), 16)
                P.wait("gpsimd", uring_free[gsl])
                uev = s_ur[gsl].inc(P, "gpsimd", lambda e, d=uring[:, gsl, :, :], s=uw_d[isub]:
                                    e.dma_start(d.rearrange("p a b -> p (a b)"), s), 16)
                pg = banks[0 + 2 * (isub % 2)]
                pu = banks[1 + 2 * (isub % 2)]
                P.wait("tensor", gev)
                P.wait("tensor", uev)
                P.wait("tensor", ln2T_ev)
                P.wait("tensor", pg.free_ev)
                P.wait("tensor", pu.free_ev)
                if il == 0:
                    P.wait("tensor", h0T_free[hsl])
                for hs in range(KS):
                    P.op("tensor", lambda e, o=pg.t[:, :], l=gring[:, gsl, hs, :],
                         r=actT[:, hs, :], st=(hs == 0), sp=(hs == KS - 1):
                         e.matmul(o, l, r, start=st, stop=sp))
                gu_ev = None
                for hs in range(KS):
                    fn = (lambda e, o=pu.t[:, :], l=uring[:, gsl, hs, :],
                          r=actT[:, hs, :], st=(hs == 0), sp=(hs == KS - 1):
                          e.matmul(o, l, r, start=st, stop=sp))
                    if hs == KS - 1:
                        gu_ev = s_pe.inc(P, "tensor", fn)
                    else:
                        P.op("tensor", fn)
                gring_free[gsl] = gu_ev
                uring_free[gsl] = gu_ev
                ssl = isub % 2
                P.wait("scalar", gu_ev)
                P.wait("scalar", silb_free[ssl])
                P.op("scalar", lambda e, o=silb[:, ssl, :], i=pg.t[:, :],
                     sc=gs[:, isub:isub + 1]: e.activation(o, i, AF.Silu, scale=sc))
                su_ev = s_sc.inc(P, "scalar", lambda e, o=ub[:, ssl, :], i=pu.t[:, :],
                                 sc=us[:, isub:isub + 1]: e.activation(o, i, AF.Identity, scale=sc))
                pg.free_ev = su_ev
                pu.free_ev = su_ev
                P.wait("vector", su_ev)
                h_ev = s_ve.inc(P, "vector", lambda e, o=h0T[:, hsl, il, :],
                                a=silb[:, ssl, :], b=ub[:, ssl, :]:
                                e.tensor_tensor(o, a, b, OP.mult))
                silb_free[ssl] = h_ev
            h_done = h_ev

            dwev = [None, None]
            for half in range(2):
                dsl_r = (ic * 2 + half) % 2
                P.wait("gpsimd", u3_free[dsl_r])
                r0 = ic * ICH * 128 + half * 512
                src = dw_d[r0:r0 + 512, :].rearrange("(a p) c -> p a c", p=128)
                dwev[half] = s_u3[dsl_r].inc(P, "gpsimd", lambda e, d=dring[:, dsl_r, :, :], s=src:
                                             e.dma_start(d, s), 16)
            P.wait("tensor", h_done)
            down_bank_rot = 0
            for t in range(G_TILES):
                pc = banks[6]
                P.wait("tensor", pc.free_ev)
                pc_ev = None
                for il in range(ICH):
                    fn = (lambda e, o=pc.t[:, 0:1], l=h0T[:, hsl, il, t * 128:(t + 1) * 128],
                          st=(il == 0), sp=(il == ICH - 1):
                          e.matmul(o, l, ones_c[:, :], start=st, stop=sp))
                    if il == ICH - 1:
                        pc_ev = s_pe.inc(P, "tensor", fn)
                    else:
                        P.op("tensor", fn)
                P.wait("vector", pc_ev)
                P.wait("vector", ones_ev)
                if first_ich[t]:
                    cv_ev = s_ve.inc(P, "vector", lambda e, o=csum[:, t:t + 1], i=pc.t[:, 0:1]:
                                     e.tensor_copy(o, i))
                else:
                    cv_ev = s_ve.inc(P, "vector", lambda e, o=csum[:, t:t + 1], i=pc.t[:, 0:1]:
                                     e.tensor_tensor(o, o, i, OP.add))
                pc.free_ev = cv_ev
                c_ready[t] = cv_ev
                for hf in range(4):
                    pd = banks[4 + down_bank_rot % 2]
                    down_bank_rot += 1
                    P.wait("tensor", pd.free_ev)
                    pd_ev = None
                    for il in range(ICH):
                        half = il // 4
                        dsl_r = (ic * 2 + half) % 2
                        P.wait("tensor", dwev[half])
                        fn = (lambda e, o=pd.t[:, :],
                              l=h0T[:, hsl, il, t * 128:(t + 1) * 128],
                              r=dring[:, dsl_r, il % 4, hf * 512:(hf + 1) * 512],
                              st=(il == 0), sp=(il == ICH - 1):
                              e.matmul(o, l, r, start=st, stop=sp))
                        if il == ICH - 1:
                            pd_ev = s_pe.inc(P, "tensor", fn)
                        else:
                            P.op("tensor", fn)
                    P.wait("vector", pd_ev)
                    ddst = dacc[:, t, hf * 512:(hf + 1) * 512]
                    if first_ich[t]:
                        a_ev = s_ve.inc(P, "vector", lambda e, o=ddst, i=pd.t[:, :]:
                                        e.tensor_copy(o, i))
                    else:
                        a_ev = s_ve.inc(P, "vector", lambda e, o=ddst, i=pd.t[:, :]:
                                        e.tensor_tensor(o, o, i, OP.add))
                    pd.free_ev = a_ev
                    dacc_ready[t] = a_ev
                first_ich[t] = False
            ich_ev = s_pe.ev()
            h0T_free[hsl] = ich_ev
            u3_free[0] = ich_ev
            u3_free[1] = ich_ev

        if stop_after == "F":
            prev_g_ve = s_ve.ev()
            prev_g_st = s_st.ev()
            continue

        P.wait("sync", s_pe.ev())
        ds_ev = None
        if "ds" not in G_SKIP:
            ds_ev = s_ds.inc(P, "sync", lambda e: e.dma_start(ds[:, :], ds_d), 16)
        for t in range(G_TILES):
            P.wait("scalar", c_ready[t])
            cn_ev = s_sc.inc(P, "scalar", lambda e, i=csum[:, t:t + 1]: e.activation(
                cneg[:, :], i, AF.Identity, scale=-128.0))
            P.wait("vector", dacc_ready[t])
            P.wait("vector", cn_ev)
            P.wait("vector", ds_ev)
            P.wait("vector", ytmp_free)
            y2_ev = s_ve.inc(P, "vector", lambda e, i=dacc[:, t, :]:
                             e.scalar_tensor_tensor(ytmp[:, :], i, cneg[:, :], ds[:, :],
                                                    OP.add, OP.mult))
            if "noln" not in G_SKIP:
                rstd3, bias3, b3_ev = layernorm(t % 2, ytmp[:, :], y2_ev, y2_ev)
            rsl = t % 2
            r1l_ev = None
            if "reload" not in G_SKIP:
                P.wait("sync", r1slot_free[rsl])
                r1l_ev = s_r1[rsl].inc(P, "sync", lambda e, sl=rsl, r0=(base_tok + t * TILE):
                                       e.dma_start(r1s[:, sl, :], r1_d[r0:r0 + 128, :]), 16)
            if "noln" not in G_SKIP:
                P.wait("scalar", b3_ev)
                n3_ev = s_sc.inc(P, "scalar", lambda e, r=rstd3, b=bias3: e.activation(
                    ytmp[:, :], ytmp[:, :], AF.Identity, scale=r, bias=b))
                P.wait("vector", n3_ev)
            else:
                P.wait("vector", y2_ev)
            P.wait("vector", r1l_ev)
            P.wait("vector", otile_free)
            o_ev = s_ve.inc(P, "vector", lambda e, a=r1s[:, rsl, :]:
                            e.tensor_tensor(otile[:, :], ytmp[:, :], a, OP.add))
            ytmp_free = o_ev
            r1slot_free[rsl] = o_ev
            if "store" not in G_SKIP:
                P.wait("sync", o_ev)
                st_ev = s_st.inc(P, "sync", lambda e, r0=(base_tok + t * TILE):
                                 e.dma_start(out_d[r0:r0 + 128, :], otile[:, :]), 16)
                otile_free = st_ev
            else:
                otile_free = o_ev
        prev_g_ve = s_ve.ev()
        prev_g_st = s_st.ev()

    P.wait("sync", s_st.ev())

    with nc.Block() as block:
        @block.tensor
        def _(e):
            P.replay("tensor", e)

        @block.vector
        def _(e):
            P.replay("vector", e)

        @block.scalar
        def _(e):
            P.replay("scalar", e)

        @block.gpsimd
        def _(e):
            P.replay("gpsimd", e)

        @block.sync
        def _(e):
            P.replay("sync", e)

    stack.close()
    return nc


def _host_layout(inputs):
    x = np.asarray(inputs["L0_attn_x"], dtype=np.float32)
    wqkv = np.ascontiguousarray(np.asarray(inputs["L0_attn_W_qkv"], dtype=np.float32))
    wout = np.ascontiguousarray(np.asarray(inputs["L0_attn_W_out"], dtype=np.float32))
    bout = np.asarray(inputs["L0_attn_b_out"], dtype=np.float32)
    gq = np.asarray(inputs["L0_ffn_gate_w"])
    uq = np.asarray(inputs["L0_ffn_up_w"])
    dq = np.asarray(inputs["L0_ffn_down_w"])
    gsc = np.asarray(inputs["L0_ffn_gate_scale"], dtype=np.float32)
    usc = np.asarray(inputs["L0_ffn_up_scale"], dtype=np.float32)
    dsc = np.asarray(inputs["L0_ffn_down_scale"], dtype=np.float32)

    def lowbyte(a):
        a = np.ascontiguousarray(a.astype("<i4", copy=False))
        return a.view(np.uint8).reshape(a.shape[0], a.shape[1], 4)[:, :, 0]

    gl = lowbyte(gq)
    ul = lowbyte(uq)
    dl = lowbyte(dq)

    g4 = gl.reshape(ISUBS, 128, KS, 128)
    gw2 = np.ascontiguousarray(g4.transpose(0, 3, 2, 1)).reshape(ISUBS, 128, D)
    u4 = ul.reshape(ISUBS, 128, KS, 128)
    uw2 = np.ascontiguousarray(u4.transpose(0, 3, 2, 1)).reshape(ISUBS, 128, D)
    dw2 = np.ascontiguousarray(dl.T)

    gs_t = np.ascontiguousarray(gsc.reshape(ISUBS, 128).T)
    us_t = np.ascontiguousarray(usc.reshape(ISUBS, 128).T)
    ds_b = np.ascontiguousarray(np.broadcast_to(dsc[None, :], (128, D)))
    bout_b = np.ascontiguousarray(np.broadcast_to(bout[None, :], (128, D)))
    ident = np.eye(128, dtype=ml_dtypes.bfloat16)

    xr = x.reshape(N_CORES * TOK_PER_CORE, D)
    shards = [np.ascontiguousarray(xr[c * TOK_PER_CORE:(c + 1) * TOK_PER_CORE])
              for c in range(N_CORES)]
    shared = dict(wqkv=wqkv, wout=wout, bout=bout_b, gw=gw2, uw=uw2, dw=dw2,
                  gs=gs_t, us=us_t, ds=ds_b, ident=ident)
    return shards, shared


def kernel(**inputs):
    shards, shared = _host_layout(inputs)
    nc = build_program()
    in_maps = [dict(x=shards[c], **shared) for c in range(N_CORES)]
    res = run_bass_kernel_spmd(nc, in_maps, core_ids=list(range(N_CORES)))
    out = np.concatenate([np.asarray(res.results[c]["out"]) for c in range(N_CORES)],
                         axis=0)
    return out.reshape(2, 8192, D).astype(np.float32)


# version 04 (final)
# speedup vs baseline: 1.1573x; 1.0011x over previous
import numpy as np
import ml_dtypes
from contextlib import ExitStack

import concourse.bass as bass
import concourse.mybir as mybir
from concourse.bass_utils import run_bass_kernel_spmd

F32 = mybir.dt.float32
BF16 = mybir.dt.bfloat16
U8 = mybir.dt.uint8
AF = mybir.ActivationFunctionType
OP = mybir.AluOpType
AX = mybir.AxisListType

D = 2048
NH = 16
HD = 128
DI = 8192
N_CORES = 8
TOK_PER_CORE = 2048
TILE = 128
G_TILES = 4
T_G = G_TILES * TILE
KS = D // 128
ISUBS = DI // 128
ICH = 8
N_ICH = ISUBS // ICH
QK_SCALE = 1.0 / np.sqrt(128.0)
EPS = 1e-5

ENGINES = ("tensor", "vector", "scalar", "gpsimd", "sync")


class Plan:

    def __init__(self):
        self.ops = {e: [] for e in ENGINES}
        self._last_wait = {}
        self.pads = {}

    def op(self, eng, fn):
        self.ops[eng].append(fn)

    def pad_op(self, eng):
        pad = self.pads.get(eng)
        if pad is None:
            return None
        if eng == "scalar":
            return lambda e, p=pad: e.copy(p, p)
        return lambda e, p=pad: e.memset(p, 0.0)

    def wait(self, eng, ev):
        if ev is None:
            return
        sem, val = ev
        if val <= 0:
            return
        key = (eng, sem.name)
        if self._last_wait.get(key, -1) >= val:
            return
        self._last_wait[key] = val
        self.ops[eng].append(lambda e, s=sem.h, v=val: e.wait_ge(s, v))
        if sem.name in ("s_pe", "s_sc", "s_ve", "s_gp") and eng in ("scalar", "vector"):
            p = self.pad_op(eng)
            if p is not None:
                self.ops[eng].append(p)

    def replay(self, eng, e):
        for fn in self.ops[eng]:
            fn(e)


class Sem:
    def __init__(self, nc, stack, name):
        self.h = stack.enter_context(nc.semaphore(name))
        self.name = name
        self.val = 0

    def inc(self, plan, eng, fn, n=1):
        self.val += n
        v = self.val
        pad = plan.pad_op(eng) if (n == 1 and eng in ("scalar", "vector")) else None
        if pad is None:
            plan.op(eng, lambda e, fn=fn, s=self.h, n=n: fn(e).then_inc(s, n))
        else:
            plan.op(eng, fn)
            plan.op(eng, lambda e, pad=pad, s=self.h, n=n: pad(e).then_inc(s, n))
        return (self, v)

    def ev(self):
        return (self, self.val)


class Bank:

    def __init__(self, t):
        self.t = t
        self.free_ev = None


DBG = set()
G_SKIP = set()


def build_program(n_groups=4, stop_after=None):
    nc = bass.Bass("TRN2", target_bir_lowering=False, detect_race_conditions=False)

    x_d = nc.dram_tensor("x", [TOK_PER_CORE, D], F32, kind="ExternalInput").ap()
    wqkv_d = nc.dram_tensor("wqkv", [D, 3 * D], F32, kind="ExternalInput").ap()
    wout_d = nc.dram_tensor("wout", [D, D], F32, kind="ExternalInput").ap()
    bout_d = nc.dram_tensor("bout", [128, D], F32, kind="ExternalInput").ap()
    gw_d = nc.dram_tensor("gw", [ISUBS, 128, D], U8, kind="ExternalInput").ap()
    uw_d = nc.dram_tensor("uw", [ISUBS, 128, D], U8, kind="ExternalInput").ap()
    dw_d = nc.dram_tensor("dw", [DI, D], U8, kind="ExternalInput").ap()
    gs_d = nc.dram_tensor("gs", [128, ISUBS], F32, kind="ExternalInput").ap()
    us_d = nc.dram_tensor("us", [128, ISUBS], F32, kind="ExternalInput").ap()
    ds_d = nc.dram_tensor("ds", [128, D], F32, kind="ExternalInput").ap()
    id_d = nc.dram_tensor("ident", [128, 128], BF16, kind="ExternalInput").ap()
    out_d = nc.dram_tensor("out", [TOK_PER_CORE, D], F32, kind="ExternalOutput").ap()
    r1_d = nc.dram_tensor("r1buf", [TOK_PER_CORE, D], F32, kind="Internal").ap()

    stack = ExitStack()
    sb = lambda name, shape, dtype: stack.enter_context(nc.sbuf_tensor(name, shape, dtype))

    def union(uname, nbytes, members):
        arena = stack.enter_context(nc.sbuf_tensor(uname, [128, nbytes], U8))
        base = nc.lookup_mloc(arena).addr
        return [nc.alloc_sbuf_tensor_at(name, list(shape), dtype, offset=base + off)
                for (name, shape, dtype, off) in members]

    ident = sb("identsb", [128, 128], BF16)
    gs = sb("gssb", [128, ISUBS], F32)
    us = sb("ussb", [128, ISUBS], F32)
    ones_c = sb("ones_c", [128, 1], BF16)

    spad = sb("spad", [128, 1], F32)
    vpad = sb("vpad", [128, 1], F32)
    P_pads = {"scalar": None, "vector": None}
    x_t = sb("x_t", [128, 2, D], F32)
    actT = sb("actT", [128, KS, T_G], BF16)
    Q = sb("Qb", [128, G_TILES, D], BF16)
    K = sb("Kb", [128, G_TILES, D], BF16)
    r1s = sb("r1s", [128, 2, D], F32)
    abf = sb("abf", [128, D], BF16)
    lnbf = sb("lnbf", [128, D], BF16)
    prodb = sb("prodb", [128, D], BF16)
    sqd = prodb
    scor = sb("scor", [128, NH * NH], F32)
    exps = sb("exps", [128, NH * NH], F32)
    Pm = sb("Pm", [128, NH * NH], F32)
    nmax = sb("nmax", [128, NH], F32)
    rden = sb("rden", [128, NH], F32)
    den = sb("den", [128, NH], F32)
    stats = sb("stats", [128, 2, 12], F32)
    csum = sb("csum", [128, G_TILES], F32)
    cneg = sb("cneg", [128, 1], F32)
    epsf = sb("epsf", [128, 1], F32)
    epsc = sb("epsc", [128, 1], F32)
    silb = sb("silb", [128, 2, T_G], BF16)
    ub = sb("ub", [128, 2, T_G], BF16)

    Vb, yb, dacc = union("u2", 32 * 1024, [
        ("Vb", [128, G_TILES, D], F32, 0),
        ("yb", [128, G_TILES, D], F32, 0),
        ("dacc", [128, G_TILES, D], F32, 0),
    ])
    h0T, gring, uring, acc, bout, ds, ytmp, otile = union("u1", 32 * 1024, [
        ("h0T", [128, 2, ICH, T_G], BF16, 0),
        ("gring", [128, 2, KS, 128], BF16, 16 * 1024),
        ("uring", [128, 2, KS, 128], BF16, 24 * 1024),
        ("accb", [128, 2, D], F32, 0),
        ("boutsb", [128, D], F32, 0),
        ("dssb", [128, D], F32, 8 * 1024),
        ("ytmp", [128, D], F32, 16 * 1024),
        ("otile", [128, D], F32, 24 * 1024),
    ])
    wqr, wor, dring = union("u3", 32 * 1024, [
        ("wqr", [128, 2, KS, 512], BF16, 0),
        ("wor", [128, 2, KS, 512], BF16, 0),
        ("dring", [128, 2, ICH // 2, D], BF16, 0),
    ])

    pb = [stack.enter_context(nc.psum_tensor(f"pb{i}", [128, 512], F32)) for i in range(8)]
    banks = [Bank(t) for t in pb[:7]]
    ptrb = Bank(pb[7])

    s_const = Sem(nc, stack, "s_const")
    s_bout = Sem(nc, stack, "s_bout")
    s_ds = Sem(nc, stack, "s_ds")
    s_x = [Sem(nc, stack, f"s_x{i}") for i in range(2)]
    s_u3 = [Sem(nc, stack, f"s_u3{i}") for i in range(2)]
    s_gr = [Sem(nc, stack, f"s_gr{i}") for i in range(2)]
    s_ur = [Sem(nc, stack, f"s_ur{i}") for i in range(2)]
    s_r1 = [Sem(nc, stack, f"s_r1{i}") for i in range(2)]
    s_pe = Sem(nc, stack, "s_pe")
    s_sc = Sem(nc, stack, "s_sc")
    s_ve = Sem(nc, stack, "s_ve")
    s_gp = Sem(nc, stack, "s_gp")
    s_st = Sem(nc, stack, "s_st")

    P = Plan()
    P.pads["scalar"] = spad[:, :]
    P.pads["vector"] = vpad[:, :]

    _init_bufs_v = [
        x_t[:, :, :], actT[:, :, :], Q[:, :, :], K[:, :, :], r1s[:, :, :],
        abf[:, :], lnbf[:, :], prodb[:, :], scor[:, :], exps[:, :], Pm[:, :],
        nmax[:, :], rden[:, :], den[:, :], stats[:, :, :], csum[:, :],
        cneg[:, :], silb[:, :, :], ub[:, :, :], Vb[:, :, :],
        ident[:, :], gs[:, :], us[:, :], ones_c[:, :],
    ]
    _init_bufs_s = [
        h0T[:, :, :, :], gring[:, :, :, :], uring[:, :, :, :],
        wqr[:, :, :, :], dring[:, :, :, :],
    ]
    for b in _init_bufs_v:
        P.op("vector", lambda e, b=b: e.memset(
            b.rearrange(" ".join(["p"] + [chr(97+i) for i in range(b.ndim-1)])
                        + " -> p (" + " ".join(chr(97+i) for i in range(b.ndim-1)) + ")")
            if b.ndim > 2 else b, 0.0))
    for b in _init_bufs_s:
        P.op("scalar", lambda e, b=b: e.memzero(
            b.rearrange(" ".join(["p"] + [chr(97+i) for i in range(b.ndim-1)])
                        + " -> p (" + " ".join(chr(97+i) for i in range(b.ndim-1)) + ")")
            if b.ndim > 2 else b))
    for bank in pb:
        P.op("vector", lambda e, b=bank: e.memset(b[:, :], 0.0))

    P.op("scalar", lambda e: e.memzero(spad[:, :]))
    iv = s_ve.inc(P, "vector", lambda e: e.memset(vpad[:, :], 0.0))
    P.wait("scalar", iv)
    iz = s_sc.inc(P, "scalar", lambda e: e.copy(spad[:, :], spad[:, :]))
    P.wait("vector", iz)
    for _ in range(80):
        P.op("scalar", P.pad_op("scalar"))
        P.op("vector", P.pad_op("vector"))
    iv2 = s_ve.inc(P, "vector", lambda e: e.tensor_copy(vpad[:, :], vpad[:, :]))
    i2 = s_sc.inc(P, "scalar", lambda e: e.copy(spad[:, :], spad[:, :]))
    P.wait("tensor", iv2)
    P.wait("sync", iv2)
    P.wait("gpsimd", iv2)
    P.wait("tensor", i2)
    P.wait("sync", i2)
    P.wait("gpsimd", i2)

    def ld(dst, src):
        return s_const.inc(P, "sync", lambda e, d=dst, s=src: e.dma_start(d, s), 16)

    ld(ident[:, :], id_d)
    ld(gs[:, :], gs_d)
    consts_ev = ld(us[:, :], us_d)
    ones_ev = s_ve.inc(P, "vector", lambda e: e.memset(ones_c[:, :], 1.0))
    P.op("vector", lambda e: e.memset(epsf[:, :], 1e-6))
    P.op("vector", lambda e: e.memset(epsc[:, :], EPS))
    P.wait("tensor", consts_ev)
    P.wait("tensor", ones_ev)
    P.wait("vector", consts_ev)
    P.wait("scalar", consts_ev)

    xslot_free = [None, None]
    accslot_free = [None, None]
    r1slot_free = [None, None]
    u3_free = [None, None]
    gring_free = [None, None]
    uring_free = [None, None]
    h0T_free = [None, None]
    silb_free = [None, None]
    lnbf_free = None
    abf_free = None
    ytmp_free = None
    otile_free = None

    def dump(name, ap, wait_ev):
        if name not in DBG:
            return None
        flat = ap if ap.ndim == 2 else ap.rearrange(
            " ".join(["p"] + [chr(97 + i) for i in range(ap.ndim - 1)])
            + " -> p (" + " ".join(chr(97 + i) for i in range(ap.ndim - 1)) + ")")
        d = nc.dram_tensor("dbg_" + name, [flat.shape[0], flat.shape[1]],
                           flat.dtype, kind="ExternalOutput").ap()
        P.wait("sync", wait_ev)
        return s_st.inc(P, "sync", lambda e, d=d, a=flat: e.dma_start(d, a), 16)

    def vreduce(fn):
        P.op("vector", P.pad_op("vector"))
        P.op("vector", fn)
        P.op("vector", P.pad_op("vector"))

    def layernorm(tl, buf_ap, src_ev_v, src_ev_s):
        sx = stats[:, tl, 0:1]
        ssq = stats[:, tl, 1:2]
        m = stats[:, tl, 2:3]
        mm = stats[:, tl, 3:4]
        negmm = stats[:, tl, 4:5]
        t0 = stats[:, tl, 5:6]
        lnv = stats[:, tl, 6:7]
        rstd = stats[:, tl, 7:8]
        bias = stats[:, tl, 8:9]
        vraw = stats[:, tl, 9:10]
        P.wait("vector", src_ev_v)
        vreduce(lambda e, o=sx, i=buf_ap: e.reduce_sum(o, i, AX.X))
        P.op("vector", lambda e, i=buf_ap: e.tensor_tensor(sqd[:, :], i, i, OP.mult))
        P.op("vector", P.pad_op("vector"))
        v_ev = s_ve.inc(P, "vector", lambda e, o=ssq: e.tensor_reduce(
            o, sqd[:, :], AX.X, OP.add))
        P.wait("scalar", v_ev)
        P.op("scalar", lambda e, o=m, i=sx: e.activation(o, i, AF.Identity, scale=1.0 / D))
        P.op("scalar", lambda e, o=mm, i=sx: e.activation(o, i, AF.Square, scale=1.0 / D))
        P.op("scalar", lambda e, o=negmm, i=mm: e.activation(o, i, AF.Identity, scale=-1.0))
        P.op("scalar", lambda e, o=t0, i=ssq: e.activation(o, i, AF.Identity,
                                                           scale=1.0 / D, bias=epsc[:, :]))
        P.op("scalar", lambda e, o=vraw, i=t0: e.activation(o, i, AF.Identity,
                                                            bias=negmm))
        P.op("scalar", lambda e, o=lnv, i=vraw: e.activation(o, i, AF.Ln, bias=epsf[:, :]))
        r_ev = s_sc.inc(P, "scalar", lambda e, o=rstd, i=lnv: e.activation(o, i, AF.Exp, scale=-0.5))
        P.wait("vector", r_ev)
        b_ev = s_ve.inc(P, "vector", lambda e, o=bias, i=m, r=rstd: e.tensor_scalar(
            o, i, -1.0, r, OP.mult, OP.mult))
        return rstd, bias, b_ev

    def transposes(src_bf_ap_fn, dst_chunk0, t_in_group, ready_ev_s):
        ev = None
        for b in range(4):
            P.wait("tensor", ready_ev_s)
            P.wait("tensor", ptrb.free_ev)
            t_ev = None
            for c in range(4):
                cc = b * 4 + c
                fn = (lambda e, o=ptrb.t[:, c * 128:(c + 1) * 128],
                      i=src_bf_ap_fn(cc), st=(c == 0), sp=(c == 3):
                      e.matmul(o, i, ident[:, :], start=st, stop=sp))
                if c == 3:
                    t_ev = s_pe.inc(P, "tensor", fn)
                else:
                    P.op("tensor", fn)
            P.wait("scalar", t_ev)
            out_ap = actT[:, b * 4:(b + 1) * 4, t_in_group * 128:(t_in_group + 1) * 128]
            ev = s_sc.inc(P, "scalar", lambda e, o=out_ap: e.copy(
                o, ptrb.t[:, :].rearrange("p (a b) -> p a b", b=128)))
            ptrb.free_ev = ev
        return ev

    ln2T_ev = None
    prev_g_ve = None
    prev_g_st = None
    for g in range(n_groups):
        base_tok = g * T_G

        ln1_evs = []
        for t in range(G_TILES):
            sl = t % 2
            P.wait("sync", xslot_free[sl])
            xev_t = s_x[sl].inc(P, "sync", lambda e, sl=sl, r0=(base_tok + t * TILE):
                                e.dma_start(x_t[:, sl, :], x_d[r0:r0 + 128, :]), 16)
            ra_v = s_ve.inc(P, "vector", lambda e: e.memset(vpad[:, :], 0.0))
            ra_s = s_sc.inc(P, "scalar", lambda e: e.copy(spad[:, :], spad[:, :]))
            P.wait("vector", ra_s)
            P.wait("scalar", ra_v)
            P.wait("tensor", ra_v)
            P.wait("tensor", ra_s)
            if t == 0:
                dump(f"g{g}xA", x_t[:, 0, :], xev_t)
            rstd, bias, b_ev = layernorm(t % 2, x_t[:, sl, :], xev_t, xev_t)
            if t == 0:
                dump(f"g{g}stA", stats[:, :, :], b_ev)
                dump(f"g{g}sqdA", sqd[:, :], b_ev)
            P.wait("scalar", b_ev)
            P.wait("scalar", lnbf_free)
            n_ev = s_sc.inc(P, "scalar", lambda e, sl=sl, r=rstd, b=bias: e.activation(
                lnbf[:, :], x_t[:, sl, :], AF.Identity, scale=r, bias=b))
            xslot_free[sl] = n_ev
            if t == 0:
                d_ev = dump(f"g{g}ln1", lnbf[:, :], n_ev)
                if d_ev is not None:
                    P.wait("tensor", d_ev)
            tr_ev = transposes(lambda c: lnbf[:, c * 128:(c + 1) * 128], None, t, n_ev)
            lnbf_free = tr_ev
            ln1_evs.append(tr_ev)
        ln1T_ev = ln1_evs[-1]
        dump(f"g{g}ln1T", actT[:, :, :], ln1T_ev)
        if stop_after == "A":
            prev_g_ve = s_ve.ev()
            prev_g_st = s_st.ev()
            continue

        qk_done = None
        v_done = None
        for n in range(12):
            sl = n % 2
            P.wait("gpsimd", u3_free[sl])
            for hq in range(4):
                src = wqkv_d[hq * 512:(hq + 1) * 512, n * 512:(n + 1) * 512]
                src = src.rearrange("(hs p) c -> p hs c", p=128)
                dst = wqr[:, sl, hq * 4:(hq + 1) * 4, :]
                wev = s_u3[sl].inc(P, "gpsimd", lambda e, d=dst, s=src: e.dma_start(d, s), 16)
            bset = [banks[(n * 4 + t) % 7] for t in range(G_TILES)]
            P.wait("tensor", wev)
            P.wait("tensor", ln1T_ev)
            for t in range(G_TILES):
                P.wait("tensor", bset[t].free_ev)
            mm_ev = None
            for hs in range(KS):
                for t in range(G_TILES):
                    lhs = actT[:, hs, t * 128:(t + 1) * 128]
                    rhs = wqr[:, sl, hs, :]
                    fn = (lambda e, o=bset[t].t[:, :], l=lhs, r=rhs,
                          st=(hs == 0), sp=(hs == KS - 1):
                          e.matmul(o, l, r, start=st, stop=sp))
                    if hs == KS - 1 and t == G_TILES - 1:
                        mm_ev = s_pe.inc(P, "tensor", fn)
                    else:
                        P.op("tensor", fn)
            u3_free[sl] = mm_ev
            if n < 8:
                P.wait("scalar", mm_ev)
                for t in range(G_TILES):
                    if n < 4:
                        dst = Q[:, t, n * 512:(n + 1) * 512]
                        e_ev = s_sc.inc(P, "scalar", lambda e, o=dst, i=bset[t].t[:, :]:
                                        e.activation(o, i, AF.Copy, scale=QK_SCALE))
                    else:
                        dst = K[:, t, (n - 4) * 512:(n - 3) * 512]
                        e_ev = s_sc.inc(P, "scalar", lambda e, o=dst, i=bset[t].t[:, :]:
                                        e.copy(o, i))
                    bset[t].free_ev = e_ev
                if n == 7:
                    qk_done = e_ev
            else:
                P.wait("vector", mm_ev)
                for t in range(G_TILES):
                    dst = Vb[:, t, (n - 8) * 512:(n - 7) * 512]
                    e_ev = s_ve.inc(P, "vector", lambda e, o=dst, i=bset[t].t[:, :]:
                                    e.tensor_copy(o, i))
                    bset[t].free_ev = e_ev
                if n == 11:
                    v_done = e_ev
        dump(f"g{g}Q", Q[:, :, :], qk_done)
        dump(f"g{g}K", K[:, :, :], qk_done)
        dump(f"g{g}V", Vb[:, :, :], v_done)
        if stop_after == "B":
            prev_g_ve = s_ve.ev()
            prev_g_st = s_st.ev()
            continue

        attnT_evs = []
        for t in range(G_TILES):
            P.wait("vector", qk_done)
            sc_ev = None
            for h in range(NH):
                q_sl = Q[:, t, h * 128:(h + 1) * 128].unsqueeze(1).broadcast_to([128, NH, 128])
                k_sl = K[:, t, :].rearrange("p (H d) -> p H d", d=128)
                pr3 = prodb[:, :].rearrange("p (H d) -> p H d", d=128)
                P.op("vector", lambda e, o=pr3, a=q_sl, b=k_sl: e.tensor_tensor(o, a, b, OP.mult))
                if h == NH - 1:
                    P.op("vector", P.pad_op("vector"))
                    sc_ev = s_ve.inc(P, "vector", lambda e, o=scor[:, h * NH:(h + 1) * NH],
                                     i=pr3: e.tensor_reduce(o, i, AX.X, OP.add))
                else:
                    vreduce(lambda e, o=scor[:, h * NH:(h + 1) * NH], i=pr3:
                            e.tensor_reduce(o, i, AX.X, OP.add))
            sc3 = scor[:, :].rearrange("p (h H) -> p h H", H=NH)
            P.op("vector", lambda e: e.memset(prodb[:, :], 0.0))
            P.op("vector", P.pad_op("vector"))
            nm_ev = s_ve.inc(P, "vector", lambda e, o=nmax[:, :], i=sc3:
                             e.reduce_max(o, i, AX.X, negate=True))
            P.wait("scalar", nm_ev)
            ex_ev = None
            for h in range(NH):
                fn = (lambda e, o=exps[:, h * NH:(h + 1) * NH],
                      i=scor[:, h * NH:(h + 1) * NH], b=nmax[:, h:h + 1],
                      a=den[:, h:h + 1]:
                      e.activation(o, i, AF.Exp, bias=b, accum_out=a))
                if h == NH - 1:
                    ex_ev = s_sc.inc(P, "scalar", fn)
                else:
                    P.op("scalar", fn)
            P.wait("vector", ex_ev)
            P.op("vector", lambda e: e.reciprocal(rden[:, :], den[:, :]))
            P.op("vector", P.pad_op("vector"))
            asl = t % 2
            P.wait("vector", v_done)
            P.wait("vector", ex_ev)
            P.wait("vector", accslot_free[asl])
            for h in range(NH):
                adst = acc[:, asl, h * 128:(h + 1) * 128]
                for H in range(NH):
                    vsl = Vb[:, t, H * 128:(H + 1) * 128]
                    pcol = exps[:, h * NH + H:h * NH + H + 1]
                    if H == 0:
                        fn = (lambda e, o=adst, v=vsl, p=pcol:
                              e.tensor_scalar(o, v, p, None, OP.mult))
                    else:
                        fn = (lambda e, o=adst, v=vsl, p=pcol:
                              e.scalar_tensor_tensor(o, v, p, o, OP.mult, OP.add))
                    P.op("vector", fn)
            P.op("vector", P.pad_op("vector"))
            av_ev = None
            for h in range(NH):
                adst = acc[:, asl, h * 128:(h + 1) * 128]
                fn = (lambda e, o=adst, r=rden[:, h:h + 1]:
                      e.tensor_scalar(o, o, r, None, OP.mult))
                if h == NH - 1:
                    av_ev = s_ve.inc(P, "vector", fn)
                else:
                    P.op("vector", fn)
            if t == 0:
                d_ev = dump(f"g{g}acc", acc[:, asl, :], av_ev)
                if d_ev is not None:
                    P.wait("scalar", d_ev)
            P.wait("scalar", av_ev)
            P.wait("scalar", abf_free)
            cast_ev = s_sc.inc(P, "scalar", lambda e, i=acc[:, asl, :]:
                               e.copy(abf[:, :], i))
            accslot_free[asl] = cast_ev
            tr_ev = transposes(lambda c: abf[:, c * 128:(c + 1) * 128], None, t, cast_ev)
            abf_free = tr_ev
            attnT_evs.append(tr_ev)
        attnT_ev = attnT_evs[-1]
        acc_done_ev = cast_ev
        if stop_after == "C":
            prev_g_ve = s_ve.ev()
            prev_g_st = s_st.ev()
            continue

        P.wait("sync", acc_done_ev)
        bout_ev = s_bout.inc(P, "sync", lambda e: e.dma_start(bout[:, :], bout_d), 16)
        for n in range(4):
            sl = n % 2
            P.wait("gpsimd", u3_free[sl])
            for hq in range(4):
                src = wout_d[hq * 512:(hq + 1) * 512, n * 512:(n + 1) * 512]
                src = src.rearrange("(hs p) c -> p hs c", p=128)
                dst = wor[:, sl, hq * 4:(hq + 1) * 4, :]
                wev = s_u3[sl].inc(P, "gpsimd", lambda e, d=dst, s=src: e.dma_start(d, s), 16)
            bset = [banks[(n * 4 + t) % 7] for t in range(G_TILES)]
            P.wait("tensor", wev)
            P.wait("tensor", attnT_ev)
            for t in range(G_TILES):
                P.wait("tensor", bset[t].free_ev)
            mm_ev = None
            for hs in range(KS):
                for t in range(G_TILES):
                    lhs = actT[:, hs, t * 128:(t + 1) * 128]
                    rhs = wor[:, sl, hs, :]
                    fn = (lambda e, o=bset[t].t[:, :], l=lhs, r=rhs,
                          st=(hs == 0), sp=(hs == KS - 1):
                          e.matmul(o, l, r, start=st, stop=sp))
                    if hs == KS - 1 and t == G_TILES - 1:
                        mm_ev = s_pe.inc(P, "tensor", fn)
                    else:
                        P.op("tensor", fn)
            u3_free[sl] = mm_ev
            P.wait("scalar", mm_ev)
            for t in range(G_TILES):
                dst = yb[:, t, n * 512:(n + 1) * 512]
                e_ev = s_sc.inc(P, "scalar", lambda e, o=dst, i=bset[t].t[:, :]: e.copy(o, i))
                bset[t].free_ev = e_ev
            y_done = e_ev

        ln2_evs = []
        for t in range(G_TILES):
            P.wait("vector", y_done)
            P.wait("vector", bout_ev)
            ba_ev = s_ve.inc(P, "vector", lambda e, y=yb[:, t, :]:
                             e.tensor_tensor(y, y, bout[:, :], OP.add))
            rstd, bias, b_ev = layernorm(t % 2, yb[:, t, :], ba_ev, ba_ev)
            if t == 0:
                d_ev = dump(f"g{g}y0", yb[:, t, :], ba_ev)
                if d_ev is not None:
                    P.wait("scalar", d_ev)
                d_ev = dump(f"g{g}st0", stats[:, :, :], b_ev)
                if d_ev is not None:
                    P.wait("scalar", d_ev)
            sl = t % 2
            P.wait("sync", xslot_free[sl])
            x2ev = s_x[sl].inc(P, "sync", lambda e, sl=sl, r0=(base_tok + t * TILE):
                               e.dma_start(x_t[:, sl, :], x_d[r0:r0 + 128, :]), 16)
            P.wait("scalar", b_ev)
            nrm_ev = s_sc.inc(P, "scalar", lambda e, y=yb[:, t, :], r=rstd, b=bias:
                              e.activation(y, y, AF.Identity, scale=r, bias=b))
            rsl = t % 2
            P.wait("vector", nrm_ev)
            P.wait("vector", x2ev)
            P.wait("vector", r1slot_free[rsl])
            r1_ev = s_ve.inc(P, "vector", lambda e, o=r1s[:, rsl, :], a=yb[:, t, :],
                             b=x_t[:, sl, :]: e.tensor_tensor(o, a, b, OP.add))
            xslot_free[sl] = r1_ev
            rstd2, bias2, b2_ev = layernorm(t % 2, r1s[:, rsl, :], r1_ev, r1_ev)
            P.wait("scalar", b2_ev)
            P.wait("scalar", lnbf_free)
            n2_ev = s_sc.inc(P, "scalar", lambda e, i=r1s[:, rsl, :], r=rstd2, b=bias2:
                             e.activation(lnbf[:, :], i, AF.Identity, scale=r, bias=b))
            tr_ev = transposes(lambda c: lnbf[:, c * 128:(c + 1) * 128], None, t, n2_ev)
            lnbf_free = tr_ev
            ln2_evs.append(tr_ev)
            P.wait("sync", tr_ev)
            sp_ev = s_r1[rsl].inc(P, "sync", lambda e, i=r1s[:, rsl, :],
                                  r0=(base_tok + t * TILE): e.dma_start(r1_d[r0:r0 + 128, :], i), 16)
            r1slot_free[rsl] = sp_ev
        ln2T_ev = ln2_evs[-1]
        if stop_after == "D":
            prev_g_ve = s_ve.ev()
            prev_g_st = s_st.ev()
            continue

        dacc_ready = [None] * G_TILES
        c_ready = [None] * G_TILES
        first_ich = [True] * G_TILES
        P.wait("gpsimd", prev_g_ve)
        P.wait("gpsimd", prev_g_st)
        for ic in range(N_ICH):
            hsl = ic % 2
            for il in range(ICH):
                isub = ic * ICH + il
                gsl = isub % 2
                P.wait("gpsimd", gring_free[gsl])
                gev = s_gr[gsl].inc(P, "gpsimd", lambda e, d=gring[:, gsl, :, :], s=gw_d[isub]:
                                    e.dma_start(d.rearrange("p a b -> p (a b)"), s), 16)
                P.wait("gpsimd", uring_free[gsl])
                uev = s_ur[gsl].inc(P, "gpsimd", lambda e, d=uring[:, gsl, :, :], s=uw_d[isub]:
                                    e.dma_start(d.rearrange("p a b -> p (a b)"), s), 16)
                pg = banks[0 + 2 * (isub % 2)]
                pu = banks[1 + 2 * (isub % 2)]
                P.wait("tensor", gev)
                P.wait("tensor", uev)
                P.wait("tensor", ln2T_ev)
                P.wait("tensor", pg.free_ev)
                P.wait("tensor", pu.free_ev)
                if il == 0:
                    P.wait("tensor", h0T_free[hsl])
                for hs in range(KS):
                    P.op("tensor", lambda e, o=pg.t[:, :], l=gring[:, gsl, hs, :],
                         r=actT[:, hs, :], st=(hs == 0), sp=(hs == KS - 1):
                         e.matmul(o, l, r, start=st, stop=sp))
                gu_ev = None
                for hs in range(KS):
                    fn = (lambda e, o=pu.t[:, :], l=uring[:, gsl, hs, :],
                          r=actT[:, hs, :], st=(hs == 0), sp=(hs == KS - 1):
                          e.matmul(o, l, r, start=st, stop=sp))
                    if hs == KS - 1:
                        gu_ev = s_pe.inc(P, "tensor", fn)
                    else:
                        P.op("tensor", fn)
                gring_free[gsl] = gu_ev
                uring_free[gsl] = gu_ev
                ssl = isub % 2
                P.wait("scalar", gu_ev)
                P.wait("scalar", silb_free[ssl])
                P.op("scalar", lambda e, o=silb[:, ssl, :], i=pg.t[:, :],
                     sc=gs[:, isub:isub + 1]: e.activation(o, i, AF.Silu, scale=sc))
                su_ev = s_sc.inc(P, "scalar", lambda e, o=ub[:, ssl, :], i=pu.t[:, :],
                                 sc=us[:, isub:isub + 1]: e.activation(o, i, AF.Identity, scale=sc))
                pg.free_ev = su_ev
                pu.free_ev = su_ev
                P.wait("vector", su_ev)
                h_ev = s_ve.inc(P, "vector", lambda e, o=h0T[:, hsl, il, :],
                                a=silb[:, ssl, :], b=ub[:, ssl, :]:
                                e.tensor_tensor(o, a, b, OP.mult))
                silb_free[ssl] = h_ev
            h_done = h_ev

            dwev = [None, None]
            for half in range(2):
                dsl_r = (ic * 2 + half) % 2
                P.wait("gpsimd", u3_free[dsl_r])
                r0 = ic * ICH * 128 + half * 512
                src = dw_d[r0:r0 + 512, :].rearrange("(a p) c -> p a c", p=128)
                dwev[half] = s_u3[dsl_r].inc(P, "gpsimd", lambda e, d=dring[:, dsl_r, :, :], s=src:
                                             e.dma_start(d, s), 16)
            P.wait("tensor", h_done)
            down_bank_rot = 0
            for t in range(G_TILES):
                pc = banks[6]
                P.wait("tensor", pc.free_ev)
                pc_ev = None
                for il in range(ICH):
                    fn = (lambda e, o=pc.t[:, 0:1], l=h0T[:, hsl, il, t * 128:(t + 1) * 128],
                          st=(il == 0), sp=(il == ICH - 1):
                          e.matmul(o, l, ones_c[:, :], start=st, stop=sp))
                    if il == ICH - 1:
                        pc_ev = s_pe.inc(P, "tensor", fn)
                    else:
                        P.op("tensor", fn)
                P.wait("vector", pc_ev)
                P.wait("vector", ones_ev)
                if first_ich[t]:
                    cv_ev = s_ve.inc(P, "vector", lambda e, o=csum[:, t:t + 1], i=pc.t[:, 0:1]:
                                     e.tensor_copy(o, i))
                else:
                    cv_ev = s_ve.inc(P, "vector", lambda e, o=csum[:, t:t + 1], i=pc.t[:, 0:1]:
                                     e.tensor_tensor(o, o, i, OP.add))
                pc.free_ev = cv_ev
                c_ready[t] = cv_ev
                for hf in range(4):
                    pd = banks[4 + down_bank_rot % 2]
                    down_bank_rot += 1
                    P.wait("tensor", pd.free_ev)
                    pd_ev = None
                    for il in range(ICH):
                        half = il // 4
                        dsl_r = (ic * 2 + half) % 2
                        P.wait("tensor", dwev[half])
                        fn = (lambda e, o=pd.t[:, :],
                              l=h0T[:, hsl, il, t * 128:(t + 1) * 128],
                              r=dring[:, dsl_r, il % 4, hf * 512:(hf + 1) * 512],
                              st=(il == 0), sp=(il == ICH - 1):
                              e.matmul(o, l, r, start=st, stop=sp))
                        if il == ICH - 1:
                            pd_ev = s_pe.inc(P, "tensor", fn)
                        else:
                            P.op("tensor", fn)
                    P.wait("vector", pd_ev)
                    ddst = dacc[:, t, hf * 512:(hf + 1) * 512]
                    if first_ich[t]:
                        a_ev = s_ve.inc(P, "vector", lambda e, o=ddst, i=pd.t[:, :]:
                                        e.tensor_copy(o, i))
                    else:
                        a_ev = s_ve.inc(P, "vector", lambda e, o=ddst, i=pd.t[:, :]:
                                        e.tensor_tensor(o, o, i, OP.add))
                    pd.free_ev = a_ev
                    dacc_ready[t] = a_ev
                first_ich[t] = False
            ich_ev = s_pe.ev()
            h0T_free[hsl] = ich_ev
            u3_free[0] = ich_ev
            u3_free[1] = ich_ev

        if stop_after == "F":
            prev_g_ve = s_ve.ev()
            prev_g_st = s_st.ev()
            continue

        P.wait("sync", s_pe.ev())
        ds_ev = None
        if "ds" not in G_SKIP:
            ds_ev = s_ds.inc(P, "sync", lambda e: e.dma_start(ds[:, :], ds_d), 16)
        for t in range(G_TILES):
            P.wait("scalar", c_ready[t])
            cn_ev = s_sc.inc(P, "scalar", lambda e, i=csum[:, t:t + 1]: e.activation(
                cneg[:, :], i, AF.Identity, scale=-128.0))
            P.wait("vector", dacc_ready[t])
            P.wait("vector", cn_ev)
            P.wait("vector", ds_ev)
            P.wait("vector", ytmp_free)
            y2_ev = s_ve.inc(P, "vector", lambda e, i=dacc[:, t, :]:
                             e.scalar_tensor_tensor(ytmp[:, :], i, cneg[:, :], ds[:, :],
                                                    OP.add, OP.mult))
            if "noln" not in G_SKIP:
                rstd3, bias3, b3_ev = layernorm(t % 2, ytmp[:, :], y2_ev, y2_ev)
            rsl = t % 2
            r1l_ev = None
            if "reload" not in G_SKIP:
                P.wait("sync", r1slot_free[rsl])
                r1l_ev = s_r1[rsl].inc(P, "sync", lambda e, sl=rsl, r0=(base_tok + t * TILE):
                                       e.dma_start(r1s[:, sl, :], r1_d[r0:r0 + 128, :]), 16)
            if "noln" not in G_SKIP:
                P.wait("scalar", b3_ev)
                n3_ev = s_sc.inc(P, "scalar", lambda e, r=rstd3, b=bias3: e.activation(
                    ytmp[:, :], ytmp[:, :], AF.Identity, scale=r, bias=b))
                P.wait("vector", n3_ev)
            else:
                P.wait("vector", y2_ev)
            P.wait("vector", r1l_ev)
            P.wait("vector", otile_free)
            o_ev = s_ve.inc(P, "vector", lambda e, a=r1s[:, rsl, :]:
                            e.tensor_tensor(otile[:, :], ytmp[:, :], a, OP.add))
            ytmp_free = o_ev
            r1slot_free[rsl] = o_ev
            if "store" not in G_SKIP:
                P.wait("sync", o_ev)
                st_ev = s_st.inc(P, "sync", lambda e, r0=(base_tok + t * TILE):
                                 e.dma_start(out_d[r0:r0 + 128, :], otile[:, :]), 16)
                otile_free = st_ev
            else:
                otile_free = o_ev
        prev_g_ve = s_ve.ev()
        prev_g_st = s_st.ev()

    P.wait("sync", s_st.ev())

    with nc.Block() as block:
        @block.tensor
        def _(e):
            P.replay("tensor", e)

        @block.vector
        def _(e):
            P.replay("vector", e)

        @block.scalar
        def _(e):
            P.replay("scalar", e)

        @block.gpsimd
        def _(e):
            P.replay("gpsimd", e)

        @block.sync
        def _(e):
            P.replay("sync", e)

    stack.close()
    return nc


def _host_layout(inputs):
    x = np.asarray(inputs["L0_attn_x"], dtype=np.float32)
    wqkv = np.ascontiguousarray(np.asarray(inputs["L0_attn_W_qkv"], dtype=np.float32))
    wout = np.ascontiguousarray(np.asarray(inputs["L0_attn_W_out"], dtype=np.float32))
    bout = np.asarray(inputs["L0_attn_b_out"], dtype=np.float32)
    gq = np.asarray(inputs["L0_ffn_gate_w"])
    uq = np.asarray(inputs["L0_ffn_up_w"])
    dq = np.asarray(inputs["L0_ffn_down_w"])
    gsc = np.asarray(inputs["L0_ffn_gate_scale"], dtype=np.float32)
    usc = np.asarray(inputs["L0_ffn_up_scale"], dtype=np.float32)
    dsc = np.asarray(inputs["L0_ffn_down_scale"], dtype=np.float32)

    def lowbyte(a):
        a = np.ascontiguousarray(a.astype("<i4", copy=False))
        return a.view(np.uint8).reshape(a.shape[0], a.shape[1], 4)[:, :, 0]

    gl = lowbyte(gq)
    ul = lowbyte(uq)
    dl = lowbyte(dq)

    g4 = gl.reshape(ISUBS, 128, KS, 128)
    gw2 = np.ascontiguousarray(g4.transpose(0, 3, 2, 1)).reshape(ISUBS, 128, D)
    u4 = ul.reshape(ISUBS, 128, KS, 128)
    uw2 = np.ascontiguousarray(u4.transpose(0, 3, 2, 1)).reshape(ISUBS, 128, D)
    dw2 = np.ascontiguousarray(dl.T)

    gs_t = np.ascontiguousarray(gsc.reshape(ISUBS, 128).T)
    us_t = np.ascontiguousarray(usc.reshape(ISUBS, 128).T)
    ds_b = np.ascontiguousarray(np.broadcast_to(dsc[None, :], (128, D)))
    bout_b = np.ascontiguousarray(np.broadcast_to(bout[None, :], (128, D)))
    ident = np.eye(128, dtype=ml_dtypes.bfloat16)

    xr = x.reshape(N_CORES * TOK_PER_CORE, D)
    shards = [np.ascontiguousarray(xr[c * TOK_PER_CORE:(c + 1) * TOK_PER_CORE])
              for c in range(N_CORES)]
    shared = dict(wqkv=wqkv, wout=wout, bout=bout_b, gw=gw2, uw=uw2, dw=dw2,
                  gs=gs_t, us=us_t, ds=ds_b, ident=ident)
    return shards, shared


def kernel(**inputs):
    shards, shared = _host_layout(inputs)
    nc = build_program()
    in_maps = [dict(x=shards[c], **shared) for c in range(N_CORES)]
    res = run_bass_kernel_spmd(nc, in_maps, core_ids=list(range(N_CORES)))
    out = np.concatenate([np.asarray(res.results[c]["out"]) for c in range(N_CORES)],
                         axis=0)
    return out.reshape(2, 8192, D).astype(np.float32)
